# version 39
# baseline (speedup 1.0000x reference)
import numpy as np
from contextlib import ExitStack
import concourse.bass as bass
import concourse.mybir as mybir
from concourse.bass_utils import run_bass_kernel_spmd

F32 = mybir.dt.float32
BF16 = mybir.dt.bfloat16
AF = mybir.ActivationFunctionType
ALU = mybir.AluOpType

S, E, H, D, FF = 2048, 1024, 16, 64, 4096
ET, ST, CH, FB, NE = 8, 16, 4, 32, 8
EXPC = 12.0
QKSCALE = float(E) ** -0.5
LN_EPS = 1e-5
NS = 3

def _snn_nspk(y):
    mem, n = 0.0, 0
    for _ in range(5):
        reset = 1.0 if (mem - 1.0 > 0) else 0.0
        mem = 0.9 * mem + y - reset
        n += 1 if (mem - 1.0 > 0) else 0
    return n

def _snn_thetas():
    brackets = [(0.24, 0.25), (0.46, 0.47), (0.68, 0.69), (0.90, 0.91), (1.08, 1.09)]
    ths = []
    for k, (lo, hi) in enumerate(brackets, start=1):
        assert _snn_nspk(lo) == k - 1 and _snn_nspk(hi) == k
        for _ in range(100):
            mid = 0.5 * (lo + hi)
            if _snn_nspk(mid) >= k:
                hi = mid
            else:
                lo = mid
        ths.append(0.5 * (lo + hi))
    return ths

THETAS = _snn_thetas()
def _silu(v):
    return v / (1.0 + np.exp(-v))
CDELT = [float(_silu(k / 5.0) - _silu((k - 1) / 5.0)) for k in range(1, 6)]

OFF_ONES = 0
OFF_CPOOL = 256
OFF_GAMMA = 288
OFF_BETA = 320
OFF_MU1 = 512
OFF_R1 = 8704
OFF_O = 16896
OFF_QT = 49664
OFF_KT = 82432
OFF_VEXT = 115200
OFF_XNT = 148480
OFF_FREE = 180864
ARENA_BYTES = 212352


def build_nc():
    nc = bass.Bass("TRN2", target_bir_lowering=False)

    xT = nc.dram_tensor("xT", [E, S], F32, kind="ExternalInput").ap()
    WqT = nc.dram_tensor("WqT", [E, E], F32, kind="ExternalInput").ap()
    WkT = nc.dram_tensor("WkT", [E, E], F32, kind="ExternalInput").ap()
    WvT = nc.dram_tensor("WvT", [E, E], F32, kind="ExternalInput").ap()
    Mh = nc.dram_tensor("Mh", [H, 128, 3968], F32, kind="ExternalInput").ap()
    gamma = nc.dram_tensor("gamma", [E, 1], F32, kind="ExternalInput").ap()
    beta = nc.dram_tensor("beta", [E, 1], F32, kind="ExternalInput").ap()
    WfcTd = nc.dram_tensor("WfcT", [E, FF], F32, kind="ExternalInput").ap()
    WprojTd = nc.dram_tensor("WprojT", [FF, E], F32, kind="ExternalInput").ap()
    outT = nc.dram_tensor("outT", [E, S], F32, kind="ExternalOutput").ap()
    ctab = nc.dram_tensor("ctab", [128, H], F32, kind="ExternalInput").ap()
    xnd = nc.dram_tensor("xnd", [128, 16384], BF16, kind="Internal").ap()

    es = ExitStack()
    arena = es.enter_context(nc.sbuf_tensor("arena", [128, ARENA_BYTES], mybir.dt.uint8))
    base = nc.lookup_mloc(arena).addr

    def vt(name, shape, dtype, off):
        h = nc.virtual_sbuf_tensor(name, shape, dtype)
        nc._tensor(h.name, h.shape, h.dtype, type="SB")
        mloc = nc.lookup_mloc(h)
        mloc.allocated = True
        mloc.addr = base + off
        return h

    ones = vt("ones", [128, 128], BF16, OFF_ONES)
    cpool = vt("cpool", [128, 8], F32, OFF_CPOOL)
    gpp = vt("gpp", [128, 8], F32, OFF_GAMMA)
    bpp = vt("bpp", [128, 8], F32, OFF_BETA)
    mu1bc = vt("mu1bc", [128, 2048], BF16, OFF_MU1)
    r1bc = vt("r1bc", [128, 2048], BF16, OFF_R1)
    xpb = [vt("xpb0", [128, 2048], BF16, OFF_QT),
           vt("xpb1", [128, 2048], BF16, OFF_QT + 4096)]
    Osb = vt("Osb", [128, 16384], BF16, OFF_O)
    Wbuf = [vt("Wbuf0", [128, 8192], BF16, OFF_O),
            vt("Wbuf1", [128, 8192], BF16, OFF_O + 16384)]
    QT = vt("QT", [128, 16384], BF16, OFF_QT)
    KT = vt("KT", [128, 16384], BF16, OFF_KT)
    xping = [vt("xping0", [128, 2048], F32, OFF_QT),
             vt("xping1", [128, 2048], F32, OFF_QT + 8192),
             vt("xping2", [128, 2048], F32, OFF_QT + 16384),
             vt("xping3", [128, 2048], F32, OFF_QT + 24576)]
    sqp = [vt("sqp0", [128, 2048], BF16, OFF_KT),
           vt("sqp1", [128, 2048], BF16, OFF_KT + 4096)]
    Vext = vt("Vext", [128, 16640], BF16, OFF_VEXT)
    xnT = vt("xnT", [128, 16384], BF16, OFF_XNT)
    Mbuf = [vt("Mbuf0", [128, 3968], BF16, OFF_XNT),
            vt("Mbuf1", [128, 3968], BF16, OFF_XNT + 7936)]
    strip = [vt("strip0", [128, 2048], BF16, OFF_XNT + 15872),
             vt("strip1", [128, 2048], BF16, OFF_XNT + 19968),
             vt("strip2", [128, 2048], BF16, OFF_FREE)]
    zrowbf = vt("zrowbf", [128, 2048], BF16, OFF_XNT + 24064)
    zrowf = vt("zrowf", [128, 2048], F32, OFF_FREE + 8192)
    zbc = vt("zbc", [128, 2048], BF16, OFF_XNT + 28160)
    oddst = vt("oddst", [128, 2048], BF16, OFF_FREE + 4096)
    p0rows = vt("p0rows", [128, 4096], BF16, OFF_FREE)
    p0f32 = vt("p0f32", [128, 2048], F32, OFF_FREE + 8192)
    WfcS = vt("WfcS", [128, 32768], BF16, OFF_QT)
    WprS = vt("WprS", [128, 32768], BF16, OFF_VEXT)
    x8 = [vt("x8a", [128, 2048], F32, OFF_FREE),
          vt("x8b", [128, 2048], F32, OFF_MU1)]
    xn8 = [vt("xn8a", [128, 2048], BF16, OFF_R1),
           vt("xn8b", [128, 2048], BF16, OFF_R1 + 4096)]
    ybuf = vt("ybuf", [128, 2048], F32, OFF_FREE + 8192)
    h1p = [vt("h1p0", [128, 256], BF16, OFF_FREE + 16384),
           vt("h1p1", [128, 256], BF16, OFF_FREE + 16896)]
    h1q = [vt("h1q0", [128, 256], BF16, OFF_FREE + 17408),
           vt("h1q1", [128, 256], BF16, OFF_FREE + 17920),
           vt("h1q2", [128, 256], BF16, OFF_FREE + 30208),
           vt("h1q3", [128, 256], BF16, OFF_FREE + 30720)]
    gacc = vt("gacc", [128, 256], BF16, OFF_FREE + 18432)
    hbuf = vt("hbuf", [128, 256], BF16, OFF_FREE + 18944)
    outt = [vt("outt0", [128, 256], F32, OFF_FREE + 19456),
            vt("outt1", [128, 256], F32, OFF_FREE + 20480)]
    t512 = vt("t512", [128, 512], F32, OFF_FREE + 21504)
    ctabs = vt("ctabs", [128, 16], F32, OFF_FREE + 31232)
    base01 = vt("base01", [128, 9], BF16, OFF_FREE + 31296)
    chpat = vt("chpat", [128, 9], BF16, OFF_FREE + 31316)
    accsb = vt("accsb", [128, 4], F32, OFF_FREE + 31336)
    s256 = vt("s256", [128, 256], BF16, OFF_FREE + 23552)
    sq256 = [vt("sq256a", [128, 256], BF16, OFF_FREE + 24064),
             vt("sq256b", [128, 256], BF16, OFF_FREE + 24576)]
    mu2c = [vt("mu2c0", [128, 256], F32, OFF_FREE + 25088),
            vt("mu2c1", [128, 256], F32, OFF_FREE + 26112)]
    r2c = [vt("r2c0", [128, 256], F32, OFF_FREE + 27136),
           vt("r2c1", [128, 256], F32, OFF_FREE + 28160)]
    rowsb = vt("rowsb", [128, 512], BF16, OFF_FREE + 29184)

    PS = es.enter_context(nc.psum_tensor("PS", [128, 4096], F32))

    sems = {}
    cnt = {}
    for name in ["dG", "dX0", "dX1", "dX2", "dX3", "dXa", "dXb", "dXN", "dWq", "dWk", "dWv",
                 "dM0", "dM1", "dFc", "dFp0", "dFp1", "dO0", "dO1", "dS",
                 "PE", "SC", "VE", "GP"]:
        sems[name] = es.enter_context(nc.semaphore(name))
        cnt[name] = 0

    ENGS = ("tensor", "vector", "scalar", "sync", "gpsimd")
    prog = {e: [] for e in ENGS}

    def q_wait(eng, sem, v):
        assert v <= cnt[sem], (eng, sem, v, cnt[sem])
        if v <= 0:
            return
        prog[eng].append(lambda e, s=sems[sem], vv=v: e.wait_ge(s, vv))

    def q(eng, meth, *args, inc=None, incn=1, **kw):
        tgt = None
        if inc is not None:
            cnt[inc] += incn
            tgt = (sems[inc], incn)
        def f(e, m=meth, a=args, k=kw, t=tgt):
            r = getattr(e, m)(*a, **k)
            if t is not None:
                r.then_inc(t[0], t[1])
        prog[eng].append(f)
        return cnt[inc] if inc else None

    def bank(b, p0=0, p1=128, c0=0, c1=512):
        return PS[p0:p1, 512 * b + c0: 512 * b + c1]

    q("vector", "memset", ones[:], 1.0)
    q("vector", "memset", cpool[:, 0:1], -EXPC, inc="VE")
    ve_consts = cnt["VE"]
    for et in range(ET):
        q("sync", "dma_start", gpp[:, et:et + 1], gamma[128 * et:128 * et + 128, 0:1], inc="dG", incn=16)
        q("sync", "dma_start", bpp[:, et:et + 1], beta[128 * et:128 * et + 128, 0:1], inc="dG", incn=16)
    q("sync", "dma_start", ctabs[:, :], ctab[0:128, :], inc="dG", incn=16)
    dg_all = cnt["dG"]
    q("vector", "memset", base01[:, :], 0.0)
    for g0 in range(3):
        for i in range(g0, 3):
            q("vector", "memset", base01[:, 3 * g0 + i:3 * g0 + i + 1], 1.0)

    ve_cast = [0] * ET
    sc_sq = [0] * ET
    pe_s2 = [0] * ET
    dx_x = [0] * ET
    for et in range(ET):
        ldq = "sync" if et % 2 == 0 else "gpsimd"
        if et >= 4:
            q_wait(ldq, "VE", ve_cast[et - 4])
        dxp = "dX%d" % (et % 4)
        dx_x[et] = q(ldq, "dma_start", xping[et % 4][:], xT[128 * et:128 * et + 128, :], inc=dxp, incn=16)
        if et == 3:
            for e2 in range(ET):
                q("gpsimd", "dma_start", Wbuf[0][:, 1024 * e2:1024 * e2 + 1024],
                  WqT[128 * e2:128 * e2 + 128, :], inc="dWq", incn=16)
        q_wait("vector", dxp, dx_x[et])
        ve_cast[et] = q("vector", "tensor_copy", xnT[:, 2048 * et:2048 * et + 2048], xping[et % 4][:], inc="VE")
        q_wait("scalar", "VE", ve_cast[et])
        if et >= 2:
            q_wait("scalar", "PE", pe_s2[et - 2])
        sc_sq[et] = q("scalar", "activation", sqp[et % 2][:], xnT[:, 2048 * et:2048 * et + 2048], AF.Square, inc="SC")
        q_wait("tensor", "VE", ve_cast[et])
        for c in range(CH):
            q("tensor", "matmul", bank(c, 0, 1), ones[:, 0:1],
              xnT[:, 2048 * et + 512 * c:2048 * et + 512 * c + 512],
              start=(et == 0), stop=(et == ET - 1))
        q_wait("tensor", "SC", sc_sq[et])
        for c in range(CH):
            q("tensor", "matmul", bank(4 + c, 0, 1), ones[:, 0:1],
              sqp[et % 2][:, 512 * c:512 * c + 512],
              start=(et == 0), stop=(et == ET - 1), inc=("PE" if c == CH - 1 else None))
        pe_s2[et] = cnt["PE"]
    pe_stats = cnt["PE"]

    q_wait("vector", "PE", pe_stats)
    for c in range(CH):
        mu = p0f32[0:1, 0:512]
        m2 = p0f32[0:1, 512:1024]
        t3 = p0f32[0:1, 1024:1536]
        va = xping[2][0:1, 512 * c:512 * c + 512]
        q("vector", "tensor_scalar", mu, bank(c, 0, 1), 1.0 / E, 0.0, op0=ALU.mult, op1=ALU.add)
        q("vector", "tensor_copy", p0rows[0:1, 512 * c:512 * c + 512], mu)
        q("vector", "tensor_scalar", m2, bank(4 + c, 0, 1), 1.0 / E, 0.0, op0=ALU.mult, op1=ALU.add)
        q("vector", "tensor_tensor", t3, mu, mu, op=ALU.mult)
        q("vector", "tensor_tensor", va, m2, t3, op=ALU.subtract)
        q("vector", "tensor_single_scalar", va, va, LN_EPS, op=ALU.add,
          inc=("VE" if c == CH - 1 else None))
    ve_rows1 = cnt["VE"]
    q_wait("scalar", "VE", ve_rows1)
    q("scalar", "activation", xping[2][0:1, 0:2048], xping[2][0:1, 0:2048], AF.Ln, inc="SC")
    sc_rt1 = q("scalar", "activation", p0rows[0:1, 2048:4096], xping[2][0:1, 0:2048],
               AF.Exp, scale=-0.5, inc="SC")

    q_wait("tensor", "VE", ve_rows1)
    q_wait("tensor", "SC", sc_rt1)
    for r in range(2):
        for c in range(CH):
            q("tensor", "matmul", bank(4 * r + c), ones[0:1, 0:128],
              p0rows[0:1, 2048 * r + 512 * c:2048 * r + 512 * c + 512],
              start=True, stop=True, inc=("PE" if (r, c) == (1, CH - 1) else None))
    pe_bc1 = cnt["PE"]
    q_wait("vector", "PE", pe_bc1)
    for c in range(CH):
        q("vector", "tensor_copy", mu1bc[:, 512 * c:512 * c + 512], bank(c))
        q("vector", "tensor_copy", r1bc[:, 512 * c:512 * c + 512], bank(4 + c), inc="VE")
    ve_bc1 = cnt["VE"]

    xn_sem = [0] * ET
    xn_eng = [0] * ET
    q_wait("gpsimd", "VE", ve_bc1)
    for et in range(ET):
        eng, scr = ("vector", xpb[0]) if et < 4 else ("gpsimd", xpb[1])
        sem = "VE" if et < 4 else "GP"
        for cc in range(4):
            sl = slice(2048 * et + 512 * cc, 2048 * et + 512 * cc + 512)
            s5 = slice(512 * cc, 512 * cc + 512)
            q(eng, "tensor_tensor", scr[:, s5], xnT[:, sl], mu1bc[:, s5], op=ALU.subtract)
            xn_sem[et] = q(eng, "tensor_tensor", xnT[:, sl], scr[:, s5], r1bc[:, s5], op=ALU.mult,
                           inc=(sem if cc == 3 else None))
        xn_eng[et] = sem
    ve_xn = xn_sem[3]
    gp_xn = xn_sem[ET - 1]

    q_wait("scalar", "VE", ve_xn)
    q_wait("scalar", "GP", gp_xn)
    dxn = q("scalar", "dma_start", xnd[0:128, :], xnT[:, :], inc="dXN", incn=16)

    dx_mark = {}
    q_wait("sync", "VE", ve_xn)
    q_wait("sync", "dXN", dxn)
    for et in range(ET):
        q("sync", "dma_start", x8[1][:, 256 * et:256 * et + 256],
          xT[128 * et:128 * et + 128, 0:256], inc="dXb", incn=16)
    for et in range(ET):
        dx_mark[0] = q("sync", "dma_start", xn8[1][:, 256 * et:256 * et + 256],
                       xnd[0:128, 2048 * et:2048 * et + 256], inc="dXb", incn=16)

    dw_sem = ["dWq", "dWk", "dWv"]
    for et in range(ET):
        q("gpsimd", "dma_start", Wbuf[1][:, 1024 * et:1024 * et + 1024], WkT[128 * et:128 * et + 128, :],
          inc="dWk", incn=16)

    q("vector", "memset", Vext[:, :], 1.0)
    q("vector", "memset", oddst[:], 0.0, inc="VE")
    ve_vones = cnt["VE"]

    pe_q = 0
    ve_qevac = {}
    def qkv_mm(widx, dst_is_q, prev_evac=None):
        nonlocal pe_q
        for j in range(ET):
            for c in range(CH):
                g = j * CH + c
                b = g % 4
                q_wait("tensor", dw_sem[widx], 16 * ET)
                if g >= 4:
                    q_wait("tensor", "VE", ve_qevac[(widx, g - 4)])
                elif prev_evac is not None:
                    q_wait("tensor", "VE", prev_evac(g))
                for et in range(ET):
                    if widx == 0 and g == 0:
                        q_wait("tensor", xn_eng[et], xn_sem[et])
                    q("tensor", "matmul", bank(b),
                      Wbuf[widx % 2][:, 1024 * et + 128 * j:1024 * et + 128 * j + 128],
                      xnT[:, 2048 * et + 512 * c:2048 * et + 512 * c + 512],
                      start=(et == 0), stop=(et == ET - 1),
                      inc=("PE" if et == ET - 1 else None))
                pe_g = cnt["PE"]
                q_wait("vector", "PE", pe_g)
                dst = QT if dst_is_q else KT
                ve_qevac[(widx, g)] = q("vector", "tensor_copy",
                                        dst[:, 2048 * j + 512 * c:2048 * j + 512 * c + 512], bank(b), inc="VE")
        pe_q = cnt["PE"]

    qkv_mm(0, True)
    q_wait("gpsimd", "PE", pe_q)
    for et in range(ET):
        q("gpsimd", "dma_start", Wbuf[0][:, 1024 * et:1024 * et + 1024], WvT[128 * et:128 * et + 128, :],
          inc="dWv", incn=16)
    qkv_mm(1, False, prev_evac=lambda g: ve_qevac[(0, 28 + g)])

    ve_vevac = {}
    q_wait("tensor", "VE", ve_vones)
    for tt in range(ST):
        for c2 in range(2):
            g = tt * 2 + c2
            b = g % 4
            q_wait("tensor", "dWv", 16 * ET)
            if g >= 4:
                q_wait("tensor", "VE", ve_vevac[g - 4])
            else:
                q_wait("tensor", "VE", ve_qevac[(1, 28 + g)])
            for et in range(ET):
                q("tensor", "matmul", bank(b),
                  xnT[:, 2048 * et + 128 * tt:2048 * et + 128 * tt + 128],
                  Wbuf[0][:, 1024 * et + 512 * c2:1024 * et + 512 * c2 + 512],
                  start=(et == 0), stop=(et == ET - 1),
                  inc=("PE" if et == ET - 1 else None))
            pe_g = cnt["PE"]
            q_wait("vector", "PE", pe_g)
            for h2 in range(8):
                h = 8 * c2 + h2
                q("vector", "tensor_copy",
                  Vext[:, 1040 * h + 65 * tt:1040 * h + 65 * tt + 64],
                  bank(b, 0, 128, 64 * h2, 64 * h2 + 64),
                  inc=("VE" if h2 == 7 else None))
            ve_vevac[g] = cnt["VE"]
    pe_qkv_done = cnt["PE"]
    ve_qkv_done = cnt["VE"]

    dw_m = [0] * H
    q_wait("gpsimd", "PE", pe_qkv_done)
    q_wait("gpsimd", "dXN", dxn)
    q("gpsimd", "dma_start", Mbuf[0][:], Mh[0], inc="dM0", incn=16)
    dw_m[0] = cnt["dM0"]
    q("gpsimd", "dma_start", Mbuf[1][:], Mh[1], inc="dM1", incn=16)
    dw_m[1] = cnt["dM1"]

    sc_exp = {}
    sc_recip = {}
    pe_qk = {}
    pe_pv = {}
    pe_bv = {}
    pe_zbc = {}
    ve_mult = {}
    ve_zcopy = {}
    ve_evac0 = {}
    ve_evac1 = {}
    ds_shift = {}
    pe_snap = {}
    ve_chpat = {}
    ve_accapply = {}
    gp_accapply = {}

    def dsthalf(hp):
        j = hp // 2
        if hp % 2 == 0:
            return Osb, 2048 * j
        return oddst, 0

    def emit_qk(g, c):
        h, tt = divmod(g, 16)
        Koff, j = 64 * (h % 2), h // 2
        if g == 0 and c == 0:
            q_wait("tensor", "VE", ve_qkv_done)
        if c < 2:
            if g >= 1:
                q_wait("tensor", "SC", sc_exp[(g - 1, c + 2)])
        else:
            q_wait("tensor", "SC", sc_exp[(g, c - 2)])
        q("tensor", "matmul", bank(c % 2),
          KT[Koff:Koff + 64, 2048 * j + 128 * tt:2048 * j + 128 * tt + 128],
          QT[Koff:Koff + 64, 2048 * j + 512 * c:2048 * j + 512 * c + 512],
          start=True, stop=True, inc="PE")
        pe_qk[(g, c)] = cnt["PE"]

    def emit_exp(g, c):
        if c == 0 and g >= NS:
            q_wait("scalar", "PE", pe_pv[g - NS])
        q_wait("scalar", "PE", pe_qk[(g, c)])
        if g == 0 and c == 0:
            q_wait("scalar", "VE", ve_consts)
        sc_exp[(g, c)] = q("scalar", "activation", strip[g % NS][:, 512 * c:512 * c + 512],
                           PS[:, 512 * (c % 2):512 * (c % 2) + 512], AF.Exp,
                           bias=cpool[:, 0:1], scale=QKSCALE, inc="SC")

    def emit_pv(gd):
        h, tt = divmod(gd, 16)
        q_wait("tensor", "SC", sc_exp[(gd, 3)])
        if tt == 0:
            q_wait("tensor", "VE", ve_mult[h - 1] if h > 0 else ve_qkv_done)
        for c in range(4):
            q("tensor", "matmul", bank(4 + c, 0, 65),
              Vext[:, 1040 * h + 65 * tt:1040 * h + 65 * tt + 65],
              strip[gd % NS][:, 512 * c:512 * c + 512],
              start=(tt == 0), stop=(tt == 15),
              inc=("PE" if c == 3 else None))
        pe_pv[gd] = cnt["PE"]

    def emit_bv_mm(hp, c, ttp, first):
        if first:
            if c < 2:
                q_wait("tensor", "VE", ve_zcopy[(hp, 1)])
                q_wait("tensor", "dM0" if hp % 2 == 0 else "dM1", dw_m[hp])
            else:
                q_wait("tensor", "GP", gp_accapply[hp])
        w0 = 512 * c - 128 * ttp + 1920
        q("tensor", "matmul", bank(2 + (c % 2), 0, 64),
          Vext[:, 1040 * hp + 65 * ttp:1040 * hp + 65 * ttp + 64],
          Mbuf[hp % 2][:, w0:w0 + 512],
          start=(ttp == 4 * c), stop=(ttp == 15), skip_group_check=True,
          inc=("PE" if (ttp == 15 and c % 2 == 1) else None))
        if ttp == 15 and c % 2 == 1:
            pe_bv[(hp, c // 2)] = cnt["PE"]

    def emit_snap_mm(hp, i):
        if i == 0:
            q_wait("tensor", "VE", ve_evac0[hp])
            q_wait("tensor", "VE", ve_chpat[hp])
        q("tensor", "matmul", bank(2, 0, 64, 0, 3),
          Vext[:, 1040 * hp + 65 * i:1040 * hp + 65 * i + 64],
          chpat[:, 3 * (i // 4):3 * (i // 4) + 3],
          start=(i == 0), stop=(i == 11), skip_group_check=True,
          inc=("PE" if i == 11 else None))
        if i == 11:
            pe_snap[hp] = cnt["PE"]
            q_wait("vector", "PE", pe_snap[hp])
            q("vector", "tensor_copy", accsb[0:64, 0:3], PS[0:64, 1024:1027], inc="VE")
            ve_accapply[hp] = cnt["VE"]
            dst, dco = dsthalf(hp)
            q_wait("gpsimd", "VE", ve_accapply[hp])
            for c in (1, 2, 3):
                q("gpsimd", "tensor_scalar", dst[0:64, dco + 512 * c:dco + 512 * c + 512],
                  dst[0:64, dco + 512 * c:dco + 512 * c + 512],
                  accsb[0:64, c - 1:c], 0.0, op0=ALU.add, op1=ALU.add,
                  inc=("GP" if c == 3 else None))
            gp_accapply[hp] = cnt["GP"]

    def emit_evac0(hp):
        dst, dco = dsthalf(hp)
        q_wait("vector", "PE", pe_bv[(hp, 0)])
        q("vector", "tensor_tensor", dst[0:64, dco + 0:dco + 512],
          dst[0:64, dco + 0:dco + 512], PS[0:64, 1024:1536], op=ALU.add)
        q("vector", "tensor_tensor", dst[0:64, dco + 512:dco + 1024],
          dst[0:64, dco + 512:dco + 1024], PS[0:64, 1536:2048], op=ALU.add, inc="VE")
        ve_evac0[hp] = cnt["VE"]

    def emit_evac1(hp):
        dst, dco = dsthalf(hp)
        q_wait("vector", "PE", pe_bv[(hp, 1)])
        q_wait("vector", "GP", gp_accapply[hp])
        q("vector", "tensor_tensor", dst[0:64, dco + 1024:dco + 1536],
          dst[0:64, dco + 1024:dco + 1536], PS[0:64, 1024:1536], op=ALU.add)
        q("vector", "tensor_tensor", dst[0:64, dco + 1536:dco + 2048],
          dst[0:64, dco + 1536:dco + 2048], PS[0:64, 1536:2048], op=ALU.add, inc="VE")
        ve_evac1[hp] = cnt["VE"]
        if hp % 2 == 1:
            q_wait("sync", "VE", ve_evac1[hp])
            jj = hp // 2
            q("sync", "dma_start", Osb[64:128, 2048 * jj:2048 * jj + 2048], oddst[0:64, :],
              inc="dS", incn=16)
            ds_shift[hp] = cnt["dS"]

    def emit_boundary_scalar(hp):
        if hp >= 1:
            emit_evac1(hp - 1)
            q_wait("vector", "PE", pe_snap[hp - 1])
        q_wait("vector", "dG", dg_all)
        q("vector", "tensor_scalar", chpat[:, :], base01[:, :],
          ctabs[:, hp:hp + 1], 0.0, op0=ALU.mult, op1=ALU.add, inc="VE")
        ve_chpat[hp] = cnt["VE"]
        q_wait("scalar", "PE", pe_pv[16 * hp + 15])
        for half in range(2):
            q("scalar", "activation", zrowf[64:65, 1024 * half:1024 * half + 1024],
              PS[64:65, 2048 + 1024 * half:3072 + 1024 * half], AF.Ln, inc="SC")
            sc_recip[(hp, half)] = q("scalar", "activation",
                                     zrowbf[64:65, 1024 * half:1024 * half + 1024],
                                     zrowf[64:65, 1024 * half:1024 * half + 1024],
                                     AF.Exp, scale=-1.0, inc="SC")

    def emit_boundary_tensor(hp):
        for pair in range(2):
            if pair == 0:
                q_wait("tensor", "SC", sc_recip[(hp, 0)])
                if hp >= 1:
                    q_wait("tensor", "VE", ve_evac1[hp - 1])
            else:
                q_wait("tensor", "SC", sc_recip[(hp, 1)])
                q_wait("tensor", "VE", ve_zcopy[(hp, 0)])
            for k in range(2):
                c = 2 * pair + k
                q("tensor", "matmul", bank(2 + k, 0, 64), ones[64:65, 0:64],
                  zrowbf[64:65, 512 * c:512 * c + 512], start=True, stop=True,
                  inc=("PE" if k == 1 else None))
            pe_zbc[(hp, pair)] = cnt["PE"]
            q_wait("vector", "PE", pe_zbc[(hp, pair)])
            for k in range(2):
                c = 2 * pair + k
                q("vector", "tensor_copy", zbc[0:64, 512 * c:512 * c + 512], bank(2 + k, 0, 64),
                  inc=("VE" if k == 1 else None))
            ve_zcopy[(hp, pair)] = cnt["VE"]
        dst, dco = dsthalf(hp)
        if hp % 2 == 1 and hp >= 3:
            q_wait("vector", "dS", ds_shift[hp - 2])
        for c in range(4):
            q("vector", "tensor_tensor", dst[0:64, dco + 512 * c:dco + 512 * c + 512],
              PS[0:64, 2048 + 512 * c:2048 + 512 * c + 512],
              zbc[0:64, 512 * c:512 * c + 512], op=ALU.mult,
              inc=("VE" if c == 3 else None))
        ve_mult[hp] = cnt["VE"]

    def emit_head_boundary(hp):
        emit_boundary_scalar(hp)
        emit_boundary_tensor(hp)

    bv_sched = {}
    snap_sched = {}
    slots0 = [2, 2, 2, 2, 3, 3, 3, 3, 4, 4, 4, 4, 5, 5, 5, 5,
              6, 6, 6, 6, 7, 7, 7, 7, 8, 8, 9, 9]
    for hp in range(H - 1):
        bb = 16 * (hp + 1)
        mms0 = [(c, t) for t in range(16) for c in (0, 1) if t >= 4 * c]
        for i, (c, t) in enumerate(mms0):
            bv_sched.setdefault(bb + slots0[i], []).append((hp, c, t, i == 0))
        for i in range(12):
            snap_sched.setdefault(bb + 10 + i // 6, []).append((hp, i))
        mms1 = [(c, t) for t in range(16) for c in (2, 3) if t >= 4 * c]
        for i, (c, t) in enumerate(mms1):
            bv_sched.setdefault(bb + 12 + i // 3, []).append((hp, c, t, i == 0))

    def pv_list(g):
        m = g % 16
        if g < 2 or m == 1:
            return []
        if m == 2:
            return [g - 2, g - 1]
        return [g - 1]

    for g in range(257):
        h, tt = divmod(g, 16)
        if g <= 255:
            if tt == 1 and h >= 1:
                emit_boundary_scalar(h - 1)
            emit_qk(g, 0)
            emit_qk(g, 1)
            emit_exp(g, 0)
            emit_exp(g, 1)
            if tt == 1 and h >= 1:
                emit_boundary_tensor(h - 1)
            if tt == 1 and 2 <= h <= 15:
                q_wait("gpsimd", "PE", pe_bv[(h - 2, 1)])
                mp = "dM0" if h % 2 == 0 else "dM1"
                q("gpsimd", "dma_start", Mbuf[h % 2][:], Mh[h], inc=mp, incn=16)
                dw_m[h] = cnt[mp]
                ft = h - 2
                if ft <= 13:
                    q("gpsimd", "dma_start", WprS[:, 1024 * ft:1024 * ft + 1024],
                      WprojTd[128 * ft:128 * ft + 128, :], inc="dFp0", incn=16)
                if h in (4, 8, 12):
                    bidx = h // 4 - 1
                    for et in (bidx, bidx + 4):
                        q("gpsimd", "dma_start", WfcS[:, 4096 * et:4096 * et + 4096],
                          WfcTd[128 * et:128 * et + 128, :], inc="dFc", incn=16)
                if h == 14:
                    for et in (3, 7):
                        q("gpsimd", "dma_start", WfcS[:, 4096 * et:4096 * et + 2048],
                          WfcTd[128 * et:128 * et + 128, 0:2048], inc="dFc", incn=16)
        for (hp, i) in snap_sched.get(g, []):
            if i == 0:
                emit_evac0(hp)
            emit_snap_mm(hp, i)
        for (hp, c, ttp, first) in bv_sched.get(g, []):
            emit_bv_mm(hp, c, ttp, first)
        if g <= 255:
            emit_qk(g, 2)
            emit_qk(g, 3)
            emit_exp(g, 2)
            emit_exp(g, 3)
        for gd in pv_list(g):
            emit_pv(gd)

    emit_head_boundary(15)
    for i, (c, t) in enumerate([(c, t) for t in range(16) for c in (0, 1) if t >= 4 * c]):
        emit_bv_mm(15, c, t, i == 0)
    emit_evac0(15)
    for i in range(12):
        emit_snap_mm(15, i)
    for i, (c, t) in enumerate([(c, t) for t in range(16) for c in (2, 3) if t >= 4 * c]):
        emit_bv_mm(15, c, t, i == 0)
    emit_evac1(15)

    attn_ve_done = cnt["VE"]
    attn_ds_done = cnt["dS"]
    attn_pe_done = cnt["PE"]
    pe_qk_last = pe_qk[(255, 3)]
    ve_mult15 = ve_mult[15]

    q_wait("gpsimd", "PE", pe_qk_last)
    for et in (3, 7):
        q("gpsimd", "dma_start", WfcS[:, 4096 * et + 2048:4096 * et + 4096],
          WfcTd[128 * et:128 * et + 128, 2048:4096], inc="dFc", incn=16)
    dw_fc = cnt["dFc"]
    q_wait("gpsimd", "PE", attn_pe_done)
    for ft in (14, 15):
        q("gpsimd", "dma_start", WprS[:, 1024 * ft:1024 * ft + 1024], WprojTd[128 * ft:128 * ft + 128, :],
          inc="dFp0", incn=16)
    dw_fp0 = cnt["dFp0"]
    dw_fp1 = None

    sc_relu = {}
    sc_rsq = {}
    ve_sq = {}
    pe_fc = {}
    pe_stats_mm = {}
    pe_bcast = {}
    ve_rows = {}
    ve_bcevac = {}
    ve_comb = {}
    ve_yevac = {}
    pe_proj = {}
    last_out = [0, 0]

    def emit_loads(k):
        bi = (k + 1) % 2
        sem = "dXa" if bi == 0 else "dXb"
        if k >= 2:
            q_wait("sync", "VE", ve_comb[k - 2])
        else:
            q_wait("sync", "VE", ve_xn)
            q_wait("sync", "dXN", dxn)
            if k == 1:
                q_wait("sync", "dS", attn_ds_done)
        for et in range(ET):
            q("sync", "dma_start", x8[bi][:, 256 * et:256 * et + 256],
              xT[128 * et:128 * et + 128, 256 * k:256 * k + 256], inc=sem, incn=16)
        for et in range(ET):
            dx_mark[k] = q("sync", "dma_start", xn8[bi][:, 256 * et:256 * et + 256],
                           xnd[0:128, 2048 * et + 256 * k:2048 * et + 256 * k + 256], inc=sem, incn=16)

    def emit_stats(k):
        for et in range(ET):
            osl = Osb[:, 2048 * et + 256 * k:2048 * et + 256 * k + 256]
            if k == 0:
                q_wait("gpsimd", "VE", ve_evac1[2 * et + 1])
                q_wait("gpsimd", "dS", ds_shift[2 * et + 1])
            if et >= 2:
                q_wait("gpsimd", "PE", pe_stats_mm[(k, et - 2)])
            gps = q("gpsimd", "tensor_tensor", sq256[et % 2][:], osl, osl,
                    op=ALU.mult, inc="GP")
            if k == 0:
                q_wait("tensor", "VE", max(ve_mult[15], ve_evac1[2 * et + 1]))
                q_wait("tensor", "dS", ds_shift[2 * et + 1])
            elif et == 0:
                q_wait("tensor", "VE", ve_rows[k - 1])
            q("tensor", "matmul", bank(6, 0, 1, 0, 256), ones[:, 0:1],
              osl,
              start=(et == 0), stop=(et == ET - 1), skip_group_check=True)
            q_wait("tensor", "GP", gps)
            q("tensor", "matmul", bank(6, 0, 1, 256, 512), ones[:, 0:1],
              sq256[et % 2][:], start=False, stop=(et == ET - 1),
              skip_group_check=True, inc="PE")
            pe_stats_mm[(k, et)] = cnt["PE"]

    def emit_rows(k):
        pi = k % 2
        ta = mu2c[pi][0:1, 0:256]
        tb = r2c[pi][0:1, 0:256]
        q_wait("vector", "PE", pe_stats_mm[(k, ET - 1)])
        if k >= 1:
            q_wait("vector", "PE", pe_bcast[k - 1])
        q("vector", "tensor_scalar", tb, bank(6, 0, 1, 0, 256),
          1.0 / E, 0.0, op0=ALU.mult, op1=ALU.add)
        q("vector", "tensor_copy", rowsb[0:1, 0:256], tb)
        q("vector", "tensor_scalar", ta, bank(6, 0, 1, 256, 512),
          1.0 / E, 0.0, op0=ALU.mult, op1=ALU.add)
        q("vector", "tensor_tensor", tb, tb, tb, op=ALU.mult)
        q("vector", "tensor_tensor", ta, ta, tb, op=ALU.subtract)
        q("vector", "tensor_single_scalar", ta, ta, LN_EPS, op=ALU.add, inc="VE")
        ve_rows[k] = cnt["VE"]
        q_wait("scalar", "VE", ve_rows[k])
        q("scalar", "activation", ta, ta, AF.Ln, inc="SC")
        sc_rsq[k] = q("scalar", "activation", rowsb[0:1, 256:512], ta, AF.Exp,
                      scale=-0.5, inc="SC")

    def emit_bcast(k):
        pi = k % 2
        q_wait("tensor", "SC", sc_rsq[k])
        if k >= 1:
            q_wait("tensor", "VE", ve_bcevac[k - 1])
        q("tensor", "matmul", bank(7, 0, 128, 0, 256), ones[0:1, 0:128],
          rowsb[0:1, 0:256], start=True, stop=True, skip_group_check=True)
        q("tensor", "matmul", bank(7, 0, 128, 256, 512), ones[0:1, 0:128],
          rowsb[0:1, 256:512], start=False, stop=True, skip_group_check=True, inc="PE")
        pe_bcast[k] = cnt["PE"]
        q_wait("vector", "PE", pe_bcast[k])
        q("vector", "tensor_copy", mu2c[pi][:, :], bank(7, 0, 128, 0, 256))
        q("vector", "tensor_copy", r2c[pi][:, :], bank(7, 0, 128, 256, 512), inc="VE")
        ve_bcevac[k] = cnt["VE"]

    def emit_fc(k, fb):
        bi = (k + 1) % 2
        if fb == 0:
            q_wait("tensor", "dFc", dw_fc)
            q_wait("tensor", "dXa" if bi == 0 else "dXb", dx_mark[k])
            if k >= 1:
                q_wait("tensor", "SC", sc_relu[(k - 1, FB - 2)])
        if fb == 1 and k >= 1:
            q_wait("tensor", "SC", sc_relu[(k - 1, FB - 1)])
        if fb >= 2:
            q_wait("tensor", "SC", sc_relu[(k, fb - 2)])
        for et in range(ET):
            q("tensor", "matmul", bank(fb % 2, 0, 128, 0, 256),
              WfcS[:, 4096 * et + 128 * fb:4096 * et + 128 * fb + 128],
              xn8[bi][:, 256 * et:256 * et + 256],
              start=(et == 0), stop=(et == ET - 1),
              inc=("PE" if et == ET - 1 else None))
        pe_fc[(k, fb)] = cnt["PE"]
        q_wait("scalar", "PE", pe_fc[(k, fb)])
        if fb >= 2:
            q_wait("scalar", "VE", ve_sq[(k, fb - 2)])
        elif k >= 1:
            q_wait("scalar", "VE", ve_sq[(k - 1, FB - 2 + fb)])
        sc_relu[(k, fb)] = q("scalar", "activation", h1p[fb % 2][:], bank(fb % 2, 0, 128, 0, 256),
                             AF.Relu, inc="SC")
        q_wait("vector", "SC", sc_relu[(k, fb)])
        ve_sq[(k, fb)] = q("vector", "tensor_tensor", h1q[fb % 4][:], h1p[fb % 2][:], h1p[fb % 2][:],
                           op=ALU.mult, inc="VE")

    def emit_proj(k, fb):
        q_wait("tensor", "VE", ve_sq[(k, fb)])
        if fb == 0:
            q_wait("tensor", "dFp0", dw_fp0)
            if k >= 1:
                q_wait("tensor", "VE", ve_yevac[k - 1])
        if fb == 16:
            q_wait("tensor", "dFp1", dw_fp1)
        for dt in range(8):
            b = 2 + dt // 2
            c0 = 256 * (dt % 2)
            q("tensor", "matmul", bank(b, 0, 128, c0, c0 + 256),
              WprS[:, 1024 * fb + 128 * dt:1024 * fb + 128 * dt + 128],
              h1q[fb % 4][:],
              start=(fb == 0 and dt % 2 == 0), stop=(fb == FB - 1),
              skip_group_check=True,
              inc=("PE" if dt == 7 else None))
        pe_proj[k] = cnt["PE"]

    def emit_combine_dt(k, dt):
        bi = (k + 1) % 2
        pi = k % 2
        if dt == 0:
            if k == 0:
                q_wait("vector", "dG", dg_all)
        q("vector", "tensor_tensor", t512[:, 0:256],
          Osb[:, 2048 * dt + 256 * k:2048 * dt + 256 * k + 256], mu2c[pi][:, :], op=ALU.subtract)
        q("vector", "tensor_tensor", t512[:, 0:256], t512[:, 0:256], r2c[pi][:, :], op=ALU.mult)
        q("vector", "tensor_scalar", s256[:], t512[:, 0:256],
          gpp[:, dt:dt + 1], bpp[:, dt:dt + 1], op0=ALU.mult, op1=ALU.add)
        ys = ybuf[:, 256 * dt:256 * dt + 256]
        q("vector", "tensor_scalar", gacc[:], ys, THETAS[0], CDELT[0], op0=ALU.is_gt, op1=ALU.mult)
        for i in range(1, 5):
            q("vector", "tensor_scalar", hbuf[:], ys, THETAS[i], CDELT[i], op0=ALU.is_gt, op1=ALU.mult)
            q("vector", "tensor_tensor", gacc[:], gacc[:], hbuf[:], op=ALU.add)
        q("vector", "tensor_tensor", hbuf[:], gacc[:], s256[:], op=ALU.mult)
        pd = "dO0" if dt % 2 == 0 else "dO1"
        if last_out[dt % 2] > 0:
            q_wait("vector", pd, last_out[dt % 2])
        ve_c = q("vector", "tensor_tensor", outt[dt % 2][:], x8[bi][:, 256 * dt:256 * dt + 256],
                 hbuf[:], op=ALU.add, inc="VE")
        q_wait("sync", "VE", ve_c)
        last_out[dt % 2] = q("sync", "dma_start",
                             outT[128 * dt:128 * dt + 128, 256 * k:256 * k + 256],
                             outt[dt % 2][:], inc=pd, incn=16)
        if dt == 7:
            ve_comb[k] = cnt["VE"]

    emit_loads(1)
    emit_stats(0)
    q_wait("gpsimd", "VE", ve_mult15)
    for ft in range(16, 32):
        q("gpsimd", "dma_start", WprS[:, 1024 * ft:1024 * ft + 1024], WprojTd[128 * ft:128 * ft + 128, :],
          inc="dFp1", incn=16)
    dw_fp1 = cnt["dFp1"]
    emit_rows(0)

    for k in range(NE):
        if k + 1 < NE:
            emit_stats(k + 1)
        for fb in range(0, 4):
            emit_fc(k, fb)
            if k == 0 and fb == 3:
                emit_bcast(0)
        for fb in range(4, FB):
            emit_fc(k, fb)
            emit_proj(k, fb - 4)
            if k >= 1 and fb >= 5 and (fb - 5) % 3 == 0 and (fb - 5) // 3 < 8:
                emit_combine_dt(k - 1, (fb - 5) // 3)
            if fb == 27 and k + 1 < NE:
                emit_rows(k + 1)
            if fb == 30:
                if k + 1 < NE:
                    emit_bcast(k + 1)
                if 1 <= k <= NE - 2:
                    emit_loads(k + 1)
        for fb in range(FB - 4, FB):
            emit_proj(k, fb)

        q_wait("vector", "PE", pe_proj[k])
        for dt in range(8):
            b = 2 + dt // 2
            c0 = 256 * (dt % 2)
            q("vector", "tensor_copy", ybuf[:, 256 * dt:256 * dt + 256], bank(b, 0, 128, c0, c0 + 256),
              inc=("VE" if dt == 7 else None))
        ve_yevac[k] = cnt["VE"]

    for dt in range(8):
        emit_combine_dt(NE - 1, dt)

    q_wait("sync", "dO0", cnt["dO0"])
    q_wait("sync", "dO1", cnt["dO1"])
    q_wait("gpsimd", "dO0", cnt["dO0"])
    q_wait("gpsimd", "dO1", cnt["dO1"])

    with es:
        with nc.allow_low_precision(reason="bf16 matmuls + fp32 accumulation sanctioned by problem tolerance"), nc.Block() as block:
            @block.tensor
            def _(e):
                for f in prog["tensor"]:
                    f(e)

            @block.vector
            def _(e):
                for f in prog["vector"]:
                    f(e)

            @block.scalar
            def _(e):
                for f in prog["scalar"]:
                    f(e)

            @block.sync
            def _(e):
                for f in prog["sync"]:
                    f(e)

            @block.gpsimd
            def _(e):
                for f in prog["gpsimd"]:
                    f(e)

    return nc


_NC_CACHE = None


def kernel(L0_attn_x, L0_attn_Wq, L0_attn_Wk, L0_attn_Wv, L0_attn_bias_table,
           L0_attn_ln_gamma, L0_attn_ln_beta, L0_ffn_W_fc, L0_ffn_W_proj):
    global _NC_CACHE
    x = np.asarray(L0_attn_x, dtype=np.float32)
    Wq = np.asarray(L0_attn_Wq, dtype=np.float32)
    Wk = np.asarray(L0_attn_Wk, dtype=np.float32)
    Wv = np.asarray(L0_attn_Wv, dtype=np.float32)
    tab = np.asarray(L0_attn_bias_table, dtype=np.float32)
    g1 = np.asarray(L0_attn_ln_gamma, dtype=np.float32)
    b1 = np.asarray(L0_attn_ln_beta, dtype=np.float32)
    Wfc = np.asarray(L0_ffn_W_fc, dtype=np.float32)
    Wp = np.asarray(L0_ffn_W_proj, dtype=np.float32)

    WqT = np.ascontiguousarray(Wq.T)
    WkT = np.ascontiguousarray(Wk.T)
    WvT = np.ascontiguousarray(Wv.T)
    WfcT = np.ascontiguousarray(Wfc.T)
    WprojT = np.ascontiguousarray(Wp.T)
    idx = np.minimum(np.arange(3968)[None, :] + 127 - np.arange(128)[:, None], 2046)
    M = np.ascontiguousarray(np.transpose(tab[idx], (2, 0, 1)))
    gcol = np.ascontiguousarray(g1.reshape(E, 1))
    bcol = np.ascontiguousarray(b1.reshape(E, 1))
    ctab_b = np.ascontiguousarray(np.broadcast_to(tab[2046:2047, :], (128, 16)))

    if _NC_CACHE is None:
        _NC_CACHE = build_nc()
    nc = _NC_CACHE

    in_maps = []
    for c in range(8):
        in_maps.append({
            "xT": np.ascontiguousarray(x[c].T),
            "WqT": WqT, "WkT": WkT, "WvT": WvT,
            "Mh": M, "gamma": gcol, "beta": bcol, "ctab": ctab_b,
            "WfcT": WfcT, "WprojT": WprojT,
        })
    res = run_bass_kernel_spmd(nc, in_maps, core_ids=list(range(8)))
    out = np.stack([np.asarray(r["outT"]).T for r in res.results], axis=0)
    return np.ascontiguousarray(out.astype(np.float32))


# version 41
# speedup vs baseline: 1.1983x; 1.1983x over previous
import numpy as np
from contextlib import ExitStack
import concourse.bass as bass
import concourse.mybir as mybir
from concourse.bass_utils import run_bass_kernel_spmd

F32 = mybir.dt.float32
BF16 = mybir.dt.bfloat16
AF = mybir.ActivationFunctionType
ALU = mybir.AluOpType

S, E, H, D, FF = 2048, 1024, 16, 64, 4096
ET, ST, CH, FB, NE = 8, 16, 4, 32, 8
EXPC = 12.0
QKSCALE = float(E) ** -0.5
LN_EPS = 1e-5
NS = 3

def _snn_nspk(y):
    mem, n = 0.0, 0
    for _ in range(5):
        reset = 1.0 if (mem - 1.0 > 0) else 0.0
        mem = 0.9 * mem + y - reset
        n += 1 if (mem - 1.0 > 0) else 0
    return n

def _snn_thetas():
    brackets = [(0.24, 0.25), (0.46, 0.47), (0.68, 0.69), (0.90, 0.91), (1.08, 1.09)]
    ths = []
    for k, (lo, hi) in enumerate(brackets, start=1):
        assert _snn_nspk(lo) == k - 1 and _snn_nspk(hi) == k
        for _ in range(100):
            mid = 0.5 * (lo + hi)
            if _snn_nspk(mid) >= k:
                hi = mid
            else:
                lo = mid
        ths.append(0.5 * (lo + hi))
    return ths

THETAS = _snn_thetas()
def _silu(v):
    return v / (1.0 + np.exp(-v))
CDELT = [float(_silu(k / 5.0) - _silu((k - 1) / 5.0)) for k in range(1, 6)]

OFF_ONES = 0
OFF_CPOOL = 256
OFF_GAMMA = 288
OFF_BETA = 320
OFF_MU1 = 512
OFF_R1 = 8704
OFF_O = 16896
OFF_QT = 49664
OFF_KT = 82432
OFF_VEXT = 115200
OFF_XNT = 148480
OFF_FREE = 180864
ARENA_BYTES = 212096


def build_nc():
    nc = bass.Bass("TRN2", target_bir_lowering=False)

    xT = nc.dram_tensor("xT", [E, S], F32, kind="ExternalInput").ap()
    WqT = nc.dram_tensor("WqT", [E, E], F32, kind="ExternalInput").ap()
    WkT = nc.dram_tensor("WkT", [E, E], F32, kind="ExternalInput").ap()
    WvT = nc.dram_tensor("WvT", [E, E], F32, kind="ExternalInput").ap()
    Mh = nc.dram_tensor("Mh", [H, 128, 3968], F32, kind="ExternalInput").ap()
    gamma = nc.dram_tensor("gamma", [E, 1], F32, kind="ExternalInput").ap()
    beta = nc.dram_tensor("beta", [E, 1], F32, kind="ExternalInput").ap()
    WfcTd = nc.dram_tensor("WfcT", [E, FF], F32, kind="ExternalInput").ap()
    WprojTd = nc.dram_tensor("WprojT", [FF, E], F32, kind="ExternalInput").ap()
    outT = nc.dram_tensor("outT", [E, S], F32, kind="ExternalOutput").ap()
    xnd = nc.dram_tensor("xnd", [128, 16384], BF16, kind="Internal").ap()

    es = ExitStack()
    arena = es.enter_context(nc.sbuf_tensor("arena", [128, ARENA_BYTES], mybir.dt.uint8))
    base = nc.lookup_mloc(arena).addr

    def vt(name, shape, dtype, off):
        h = nc.virtual_sbuf_tensor(name, shape, dtype)
        nc._tensor(h.name, h.shape, h.dtype, type="SB")
        mloc = nc.lookup_mloc(h)
        mloc.allocated = True
        mloc.addr = base + off
        return h

    ones = vt("ones", [128, 128], BF16, OFF_ONES)
    cpool = vt("cpool", [128, 8], F32, OFF_CPOOL)
    gpp = vt("gpp", [128, 8], F32, OFF_GAMMA)
    bpp = vt("bpp", [128, 8], F32, OFF_BETA)
    mu1bc = vt("mu1bc", [128, 2048], BF16, OFF_MU1)
    r1bc = vt("r1bc", [128, 2048], BF16, OFF_R1)
    xpb = [vt("xpb0", [128, 2048], BF16, OFF_QT),
           vt("xpb1", [128, 2048], BF16, OFF_QT + 4096)]
    Osb = vt("Osb", [128, 16384], BF16, OFF_O)
    Wbuf = [vt("Wbuf0", [128, 8192], BF16, OFF_O),
            vt("Wbuf1", [128, 8192], BF16, OFF_O + 16384)]
    QT = vt("QT", [128, 16384], BF16, OFF_QT)
    KT = vt("KT", [128, 16384], BF16, OFF_KT)
    xping = [vt("xping0", [128, 2048], F32, OFF_QT),
             vt("xping1", [128, 2048], F32, OFF_QT + 8192),
             vt("xping2", [128, 2048], F32, OFF_QT + 16384),
             vt("xping3", [128, 2048], F32, OFF_QT + 24576)]
    sqp = [vt("sqp0", [128, 2048], BF16, OFF_KT),
           vt("sqp1", [128, 2048], BF16, OFF_KT + 4096)]
    Vext = vt("Vext", [128, 16640], BF16, OFF_VEXT)
    xnT = vt("xnT", [128, 16384], BF16, OFF_XNT)
    Mbuf = [vt("Mbuf0", [128, 3968], BF16, OFF_XNT),
            vt("Mbuf1", [128, 3968], BF16, OFF_XNT + 7936)]
    strip = [vt("strip0", [128, 2048], BF16, OFF_XNT + 15872),
             vt("strip1", [128, 2048], BF16, OFF_XNT + 19968),
             vt("strip2", [128, 2048], BF16, OFF_FREE)]
    zrowbf = vt("zrowbf", [128, 2048], BF16, OFF_XNT + 24064)
    zrowf = vt("zrowf", [128, 2048], F32, OFF_FREE + 8192)
    zbc = vt("zbc", [128, 2048], BF16, OFF_XNT + 28160)
    oddst = vt("oddst", [128, 2048], BF16, OFF_FREE + 4096)
    p0rows = vt("p0rows", [128, 4096], BF16, OFF_FREE)
    p0f32 = vt("p0f32", [128, 2048], F32, OFF_FREE + 8192)
    WfcS = vt("WfcS", [128, 32768], BF16, OFF_QT)
    WprS = vt("WprS", [128, 32768], BF16, OFF_VEXT)
    x8 = [vt("x8a", [128, 2048], F32, OFF_FREE),
          vt("x8b", [128, 2048], F32, OFF_MU1)]
    xn8 = [vt("xn8a", [128, 2048], BF16, OFF_R1),
           vt("xn8b", [128, 2048], BF16, OFF_R1 + 4096)]
    ybuf = vt("ybuf", [128, 2048], F32, OFF_FREE + 8192)
    h1p = [vt("h1p0", [128, 256], BF16, OFF_FREE + 16384),
           vt("h1p1", [128, 256], BF16, OFF_FREE + 16896)]
    h1q = [vt("h1q0", [128, 256], BF16, OFF_FREE + 17408),
           vt("h1q1", [128, 256], BF16, OFF_FREE + 17920),
           vt("h1q2", [128, 256], BF16, OFF_FREE + 30208),
           vt("h1q3", [128, 256], BF16, OFF_FREE + 30720)]
    gacc = vt("gacc", [128, 256], BF16, OFF_FREE + 18432)
    hbuf = vt("hbuf", [128, 256], BF16, OFF_FREE + 18944)
    outt = [vt("outt0", [128, 256], F32, OFF_FREE + 19456),
            vt("outt1", [128, 256], F32, OFF_FREE + 20480)]
    t512 = vt("t512", [128, 512], F32, OFF_FREE + 21504)
    s256 = vt("s256", [128, 256], BF16, OFF_FREE + 23552)
    sq256 = [vt("sq256a", [128, 256], BF16, OFF_FREE + 24064),
             vt("sq256b", [128, 256], BF16, OFF_FREE + 24576)]
    mu2c = [vt("mu2c0", [128, 256], F32, OFF_FREE + 25088),
            vt("mu2c1", [128, 256], F32, OFF_FREE + 26112)]
    r2c = [vt("r2c0", [128, 256], F32, OFF_FREE + 27136),
           vt("r2c1", [128, 256], F32, OFF_FREE + 28160)]
    rowsb = vt("rowsb", [128, 512], BF16, OFF_FREE + 29184)

    PS = es.enter_context(nc.psum_tensor("PS", [128, 4096], F32))

    sems = {}
    cnt = {}
    for name in ["dG", "dX0", "dX1", "dX2", "dX3", "dXa", "dXb", "dXN", "dWq", "dWk", "dWv",
                 "dM0", "dM1", "dFc", "dFp0", "dFp1", "dO0", "dO1", "dS",
                 "PE", "SC", "VE", "GP"]:
        sems[name] = es.enter_context(nc.semaphore(name))
        cnt[name] = 0

    ENGS = ("tensor", "vector", "scalar", "sync", "gpsimd")
    prog = {e: [] for e in ENGS}

    def q_wait(eng, sem, v):
        assert v <= cnt[sem], (eng, sem, v, cnt[sem])
        if v <= 0:
            return
        prog[eng].append(lambda e, s=sems[sem], vv=v: e.wait_ge(s, vv))

    def q(eng, meth, *args, inc=None, incn=1, **kw):
        tgt = None
        if inc is not None:
            cnt[inc] += incn
            tgt = (sems[inc], incn)
        def f(e, m=meth, a=args, k=kw, t=tgt):
            r = getattr(e, m)(*a, **k)
            if t is not None:
                r.then_inc(t[0], t[1])
        prog[eng].append(f)
        return cnt[inc] if inc else None

    def bank(b, p0=0, p1=128, c0=0, c1=512):
        return PS[p0:p1, 512 * b + c0: 512 * b + c1]

    q("vector", "memset", ones[:], 1.0)
    q("vector", "memset", cpool[:, 0:1], -EXPC, inc="VE")
    ve_consts = cnt["VE"]
    for et in range(ET):
        q("sync", "dma_start", gpp[:, et:et + 1], gamma[128 * et:128 * et + 128, 0:1], inc="dG", incn=16)
        q("sync", "dma_start", bpp[:, et:et + 1], beta[128 * et:128 * et + 128, 0:1], inc="dG", incn=16)
    dg_all = cnt["dG"]

    ve_cast = [0] * ET
    sc_sq = [0] * ET
    pe_s2 = [0] * ET
    dx_x = [0] * ET
    for et in range(ET):
        ldq = "sync" if et % 2 == 0 else "gpsimd"
        if et >= 4:
            q_wait(ldq, "VE", ve_cast[et - 4])
        dxp = "dX%d" % (et % 4)
        dx_x[et] = q(ldq, "dma_start", xping[et % 4][:], xT[128 * et:128 * et + 128, :], inc=dxp, incn=16)
        if et == 3:
            for e2 in range(ET):
                q("gpsimd", "dma_start", Wbuf[0][:, 1024 * e2:1024 * e2 + 1024],
                  WqT[128 * e2:128 * e2 + 128, :], inc="dWq", incn=16)
        q_wait("vector", dxp, dx_x[et])
        ve_cast[et] = q("vector", "tensor_copy", xnT[:, 2048 * et:2048 * et + 2048], xping[et % 4][:], inc="VE")
        q_wait("scalar", "VE", ve_cast[et])
        if et >= 2:
            q_wait("scalar", "PE", pe_s2[et - 2])
        sc_sq[et] = q("scalar", "activation", sqp[et % 2][:], xnT[:, 2048 * et:2048 * et + 2048], AF.Square, inc="SC")
        q_wait("tensor", "VE", ve_cast[et])
        for c in range(CH):
            q("tensor", "matmul", bank(c, 0, 1), ones[:, 0:1],
              xnT[:, 2048 * et + 512 * c:2048 * et + 512 * c + 512],
              start=(et == 0), stop=(et == ET - 1))
        q_wait("tensor", "SC", sc_sq[et])
        for c in range(CH):
            q("tensor", "matmul", bank(4 + c, 0, 1), ones[:, 0:1],
              sqp[et % 2][:, 512 * c:512 * c + 512],
              start=(et == 0), stop=(et == ET - 1), inc=("PE" if c == CH - 1 else None))
        pe_s2[et] = cnt["PE"]
    pe_stats = cnt["PE"]

    q_wait("vector", "PE", pe_stats)
    for c in range(CH):
        mu = p0f32[0:1, 0:512]
        m2 = p0f32[0:1, 512:1024]
        t3 = p0f32[0:1, 1024:1536]
        va = xping[2][0:1, 512 * c:512 * c + 512]
        q("vector", "tensor_scalar", mu, bank(c, 0, 1), 1.0 / E, 0.0, op0=ALU.mult, op1=ALU.add)
        q("vector", "tensor_copy", p0rows[0:1, 512 * c:512 * c + 512], mu)
        q("vector", "tensor_scalar", m2, bank(4 + c, 0, 1), 1.0 / E, 0.0, op0=ALU.mult, op1=ALU.add)
        q("vector", "tensor_tensor", t3, mu, mu, op=ALU.mult)
        q("vector", "tensor_tensor", va, m2, t3, op=ALU.subtract)
        q("vector", "tensor_single_scalar", va, va, LN_EPS, op=ALU.add,
          inc=("VE" if c == CH - 1 else None))
    ve_rows1 = cnt["VE"]
    q_wait("scalar", "VE", ve_rows1)
    q("scalar", "activation", xping[2][0:1, 0:2048], xping[2][0:1, 0:2048], AF.Ln, inc="SC")
    sc_rt1 = q("scalar", "activation", p0rows[0:1, 2048:4096], xping[2][0:1, 0:2048],
               AF.Exp, scale=-0.5, inc="SC")

    q_wait("tensor", "VE", ve_rows1)
    q_wait("tensor", "SC", sc_rt1)
    for r in range(2):
        for c in range(CH):
            q("tensor", "matmul", bank(4 * r + c), ones[0:1, 0:128],
              p0rows[0:1, 2048 * r + 512 * c:2048 * r + 512 * c + 512],
              start=True, stop=True, inc=("PE" if (r, c) == (1, CH - 1) else None))
    pe_bc1 = cnt["PE"]
    q_wait("vector", "PE", pe_bc1)
    for c in range(CH):
        q("vector", "tensor_copy", mu1bc[:, 512 * c:512 * c + 512], bank(c))
        q("vector", "tensor_copy", r1bc[:, 512 * c:512 * c + 512], bank(4 + c), inc="VE")
    ve_bc1 = cnt["VE"]

    xn_sem = [0] * ET
    xn_eng = [0] * ET
    q_wait("gpsimd", "VE", ve_bc1)
    for et in range(ET):
        eng, scr = ("vector", xpb[0]) if et < 4 else ("gpsimd", xpb[1])
        sem = "VE" if et < 4 else "GP"
        for cc in range(4):
            sl = slice(2048 * et + 512 * cc, 2048 * et + 512 * cc + 512)
            s5 = slice(512 * cc, 512 * cc + 512)
            q(eng, "tensor_tensor", scr[:, s5], xnT[:, sl], mu1bc[:, s5], op=ALU.subtract)
            xn_sem[et] = q(eng, "tensor_tensor", xnT[:, sl], scr[:, s5], r1bc[:, s5], op=ALU.mult,
                           inc=(sem if cc == 3 else None))
        xn_eng[et] = sem
    ve_xn = xn_sem[3]
    gp_xn = xn_sem[ET - 1]

    q_wait("scalar", "VE", ve_xn)
    q_wait("scalar", "GP", gp_xn)
    dxn = q("scalar", "dma_start", xnd[0:128, :], xnT[:, :], inc="dXN", incn=16)

    dx_mark = {}
    q_wait("sync", "VE", ve_xn)
    q_wait("sync", "dXN", dxn)
    for et in range(ET):
        q("sync", "dma_start", x8[1][:, 256 * et:256 * et + 256],
          xT[128 * et:128 * et + 128, 0:256], inc="dXb", incn=16)
    for et in range(ET):
        dx_mark[0] = q("sync", "dma_start", xn8[1][:, 256 * et:256 * et + 256],
                       xnd[0:128, 2048 * et:2048 * et + 256], inc="dXb", incn=16)

    dw_sem = ["dWq", "dWk", "dWv"]
    for et in range(ET):
        q("gpsimd", "dma_start", Wbuf[1][:, 1024 * et:1024 * et + 1024], WkT[128 * et:128 * et + 128, :],
          inc="dWk", incn=16)

    q("vector", "memset", Vext[:, :], 1.0)
    q("vector", "memset", oddst[:], 0.0, inc="VE")
    ve_vones = cnt["VE"]

    pe_q = 0
    ve_qevac = {}
    def qkv_mm(widx, dst_is_q, prev_evac=None):
        nonlocal pe_q
        for j in range(ET):
            for c in range(CH):
                g = j * CH + c
                b = g % 4
                q_wait("tensor", dw_sem[widx], 16 * ET)
                if g >= 4:
                    q_wait("tensor", "VE", ve_qevac[(widx, g - 4)])
                elif prev_evac is not None:
                    q_wait("tensor", "VE", prev_evac(g))
                for et in range(ET):
                    if widx == 0 and g == 0:
                        q_wait("tensor", xn_eng[et], xn_sem[et])
                    q("tensor", "matmul", bank(b),
                      Wbuf[widx % 2][:, 1024 * et + 128 * j:1024 * et + 128 * j + 128],
                      xnT[:, 2048 * et + 512 * c:2048 * et + 512 * c + 512],
                      start=(et == 0), stop=(et == ET - 1),
                      inc=("PE" if et == ET - 1 else None))
                pe_g = cnt["PE"]
                q_wait("vector", "PE", pe_g)
                dst = QT if dst_is_q else KT
                ve_qevac[(widx, g)] = q("vector", "tensor_copy",
                                        dst[:, 2048 * j + 512 * c:2048 * j + 512 * c + 512], bank(b), inc="VE")
        pe_q = cnt["PE"]

    qkv_mm(0, True)
    q_wait("gpsimd", "PE", pe_q)
    for et in range(ET):
        q("gpsimd", "dma_start", Wbuf[0][:, 1024 * et:1024 * et + 1024], WvT[128 * et:128 * et + 128, :],
          inc="dWv", incn=16)
    qkv_mm(1, False, prev_evac=lambda g: ve_qevac[(0, 28 + g)])

    ve_vevac = {}
    q_wait("tensor", "VE", ve_vones)
    for tt in range(ST):
        for c2 in range(2):
            g = tt * 2 + c2
            b = g % 4
            q_wait("tensor", "dWv", 16 * ET)
            if g >= 4:
                q_wait("tensor", "VE", ve_vevac[g - 4])
            else:
                q_wait("tensor", "VE", ve_qevac[(1, 28 + g)])
            for et in range(ET):
                q("tensor", "matmul", bank(b),
                  xnT[:, 2048 * et + 128 * tt:2048 * et + 128 * tt + 128],
                  Wbuf[0][:, 1024 * et + 512 * c2:1024 * et + 512 * c2 + 512],
                  start=(et == 0), stop=(et == ET - 1),
                  inc=("PE" if et == ET - 1 else None))
            pe_g = cnt["PE"]
            q_wait("vector", "PE", pe_g)
            for h2 in range(8):
                h = 8 * c2 + h2
                q("vector", "tensor_copy",
                  Vext[:, 1040 * h + 65 * tt:1040 * h + 65 * tt + 64],
                  bank(b, 0, 128, 64 * h2, 64 * h2 + 64),
                  inc=("VE" if h2 == 7 else None))
            ve_vevac[g] = cnt["VE"]
    pe_qkv_done = cnt["PE"]
    ve_qkv_done = cnt["VE"]

    dw_m = [0] * H
    q_wait("gpsimd", "PE", pe_qkv_done)
    q_wait("gpsimd", "dXN", dxn)
    q("gpsimd", "dma_start", Mbuf[0][:], Mh[0], inc="dM0", incn=16)
    dw_m[0] = cnt["dM0"]
    q("gpsimd", "dma_start", Mbuf[1][:], Mh[1], inc="dM1", incn=16)
    dw_m[1] = cnt["dM1"]

    sc_exp = {}
    sc_recip = {}
    pe_qk = {}
    pe_pv = {}
    pe_bv = {}
    pe_zbc = {}
    ve_mult = {}
    ve_zcopy = {}
    ve_evac0 = {}
    ve_evac1 = {}
    ds_shift = {}

    def dsthalf(hp):
        j = hp // 2
        if hp % 2 == 0:
            return Osb, 2048 * j
        return oddst, 0

    def emit_qk(g, c):
        h, tt = divmod(g, 16)
        Koff, j = 64 * (h % 2), h // 2
        if g == 0 and c == 0:
            q_wait("tensor", "VE", ve_qkv_done)
        if c < 2:
            if g >= 1:
                q_wait("tensor", "SC", sc_exp[(g - 1, c + 2)])
        else:
            q_wait("tensor", "SC", sc_exp[(g, c - 2)])
        q("tensor", "matmul", bank(c % 2),
          KT[Koff:Koff + 64, 2048 * j + 128 * tt:2048 * j + 128 * tt + 128],
          QT[Koff:Koff + 64, 2048 * j + 512 * c:2048 * j + 512 * c + 512],
          start=True, stop=True, inc="PE")
        pe_qk[(g, c)] = cnt["PE"]

    def emit_exp(g, c):
        if c == 0 and g >= NS:
            q_wait("scalar", "PE", pe_pv[g - NS])
        q_wait("scalar", "PE", pe_qk[(g, c)])
        if g == 0 and c == 0:
            q_wait("scalar", "VE", ve_consts)
        sc_exp[(g, c)] = q("scalar", "activation", strip[g % NS][:, 512 * c:512 * c + 512],
                           PS[:, 512 * (c % 2):512 * (c % 2) + 512], AF.Exp,
                           bias=cpool[:, 0:1], scale=QKSCALE, inc="SC")

    def emit_pv(gd):
        h, tt = divmod(gd, 16)
        q_wait("tensor", "SC", sc_exp[(gd, 3)])
        if tt == 0:
            q_wait("tensor", "VE", ve_mult[h - 1] if h > 0 else ve_qkv_done)
        for c in range(4):
            q("tensor", "matmul", bank(4 + c, 0, 65),
              Vext[:, 1040 * h + 65 * tt:1040 * h + 65 * tt + 65],
              strip[gd % NS][:, 512 * c:512 * c + 512],
              start=(tt == 0), stop=(tt == 15),
              inc=("PE" if c == 3 else None))
        pe_pv[gd] = cnt["PE"]

    def emit_bv_mm(hp, c, ttp, first):
        if first:
            if c < 2:
                q_wait("tensor", "VE", ve_zcopy[(hp, 1)])
                q_wait("tensor", "dM0" if hp % 2 == 0 else "dM1", dw_m[hp])
            else:
                q_wait("tensor", "VE", ve_evac0[hp])
        w0 = 512 * c - 128 * ttp + 1920
        q("tensor", "matmul", bank(2 + (c % 2), 0, 64),
          Vext[:, 1040 * hp + 65 * ttp:1040 * hp + 65 * ttp + 64],
          Mbuf[hp % 2][:, w0:w0 + 512],
          start=(ttp == 0), stop=(ttp == 15),
          inc=("PE" if (ttp == 15 and c % 2 == 1) else None))
        if ttp == 15 and c % 2 == 1:
            pe_bv[(hp, c // 2)] = cnt["PE"]

    def emit_evac0(hp):
        dst, dco = dsthalf(hp)
        q_wait("vector", "PE", pe_bv[(hp, 0)])
        q("vector", "tensor_tensor", dst[0:64, dco + 0:dco + 512],
          dst[0:64, dco + 0:dco + 512], PS[0:64, 1024:1536], op=ALU.add)
        q("vector", "tensor_tensor", dst[0:64, dco + 512:dco + 1024],
          dst[0:64, dco + 512:dco + 1024], PS[0:64, 1536:2048], op=ALU.add, inc="VE")
        ve_evac0[hp] = cnt["VE"]

    def emit_evac1(hp):
        dst, dco = dsthalf(hp)
        q_wait("vector", "PE", pe_bv[(hp, 1)])
        q("vector", "tensor_tensor", dst[0:64, dco + 1024:dco + 1536],
          dst[0:64, dco + 1024:dco + 1536], PS[0:64, 1024:1536], op=ALU.add)
        q("vector", "tensor_tensor", dst[0:64, dco + 1536:dco + 2048],
          dst[0:64, dco + 1536:dco + 2048], PS[0:64, 1536:2048], op=ALU.add, inc="VE")
        ve_evac1[hp] = cnt["VE"]
        if hp % 2 == 1:
            q_wait("sync", "VE", ve_evac1[hp])
            jj = hp // 2
            q("sync", "dma_start", Osb[64:128, 2048 * jj:2048 * jj + 2048], oddst[0:64, :],
              inc="dS", incn=16)
            ds_shift[hp] = cnt["dS"]

    def emit_boundary_scalar(hp):
        if hp >= 1:
            emit_evac1(hp - 1)
        q_wait("scalar", "PE", pe_pv[16 * hp + 15])
        for half in range(2):
            q("scalar", "activation", zrowf[64:65, 1024 * half:1024 * half + 1024],
              PS[64:65, 2048 + 1024 * half:3072 + 1024 * half], AF.Ln, inc="SC")
            sc_recip[(hp, half)] = q("scalar", "activation",
                                     zrowbf[64:65, 1024 * half:1024 * half + 1024],
                                     zrowf[64:65, 1024 * half:1024 * half + 1024],
                                     AF.Exp, scale=-1.0, inc="SC")

    def emit_boundary_tensor(hp):
        for pair in range(2):
            if pair == 0:
                q_wait("tensor", "SC", sc_recip[(hp, 0)])
                if hp >= 1:
                    q_wait("tensor", "VE", ve_evac1[hp - 1])
            else:
                q_wait("tensor", "SC", sc_recip[(hp, 1)])
                q_wait("tensor", "VE", ve_zcopy[(hp, 0)])
            for k in range(2):
                c = 2 * pair + k
                q("tensor", "matmul", bank(2 + k, 0, 64), ones[64:65, 0:64],
                  zrowbf[64:65, 512 * c:512 * c + 512], start=True, stop=True,
                  inc=("PE" if k == 1 else None))
            pe_zbc[(hp, pair)] = cnt["PE"]
            q_wait("vector", "PE", pe_zbc[(hp, pair)])
            for k in range(2):
                c = 2 * pair + k
                q("vector", "tensor_copy", zbc[0:64, 512 * c:512 * c + 512], bank(2 + k, 0, 64),
                  inc=("VE" if k == 1 else None))
            ve_zcopy[(hp, pair)] = cnt["VE"]
        dst, dco = dsthalf(hp)
        if hp % 2 == 1 and hp >= 3:
            q_wait("vector", "dS", ds_shift[hp - 2])
        for c in range(4):
            q("vector", "tensor_tensor", dst[0:64, dco + 512 * c:dco + 512 * c + 512],
              PS[0:64, 2048 + 512 * c:2048 + 512 * c + 512],
              zbc[0:64, 512 * c:512 * c + 512], op=ALU.mult,
              inc=("VE" if c == 3 else None))
        ve_mult[hp] = cnt["VE"]

    def emit_head_boundary(hp):
        emit_boundary_scalar(hp)
        emit_boundary_tensor(hp)

    bv_sched = {}
    for hp in range(H - 1):
        bb = 16 * (hp + 1)
        mms0 = [(c, t) for t in range(16) for c in (0, 1)]
        for i, (c, t) in enumerate(mms0):
            bv_sched.setdefault(bb + 2 + i // 4, []).append((hp, c, t, i == 0))
        mms1 = [(c, t) for t in range(16) for c in (2, 3)]
        for i, (c, t) in enumerate(mms1):
            gc = bb + 10 + i // 5 if i < 30 else bb + 16
            bv_sched.setdefault(gc, []).append((hp, c, t, i == 0))

    def pv_list(g):
        m = g % 16
        if g < 2 or m == 1:
            return []
        if m == 2:
            return [g - 2, g - 1]
        return [g - 1]

    for g in range(257):
        h, tt = divmod(g, 16)
        if g <= 255:
            if tt == 1 and h >= 1:
                emit_boundary_scalar(h - 1)
            emit_qk(g, 0)
            emit_qk(g, 1)
            emit_exp(g, 0)
            emit_exp(g, 1)
            if tt == 1 and h >= 1:
                emit_boundary_tensor(h - 1)
            if tt == 1 and 2 <= h <= 15:
                q_wait("gpsimd", "PE", pe_bv[(h - 2, 1)])
                mp = "dM0" if h % 2 == 0 else "dM1"
                q("gpsimd", "dma_start", Mbuf[h % 2][:], Mh[h], inc=mp, incn=16)
                dw_m[h] = cnt[mp]
                ft = h - 2
                if ft <= 13:
                    q("gpsimd", "dma_start", WprS[:, 1024 * ft:1024 * ft + 1024],
                      WprojTd[128 * ft:128 * ft + 128, :], inc="dFp0", incn=16)
                if h in (4, 8, 12):
                    bidx = h // 4 - 1
                    for et in (bidx, bidx + 4):
                        q("gpsimd", "dma_start", WfcS[:, 4096 * et:4096 * et + 4096],
                          WfcTd[128 * et:128 * et + 128, :], inc="dFc", incn=16)
                if h == 14:
                    for et in (3, 7):
                        q("gpsimd", "dma_start", WfcS[:, 4096 * et:4096 * et + 2048],
                          WfcTd[128 * et:128 * et + 128, 0:2048], inc="dFc", incn=16)
        for (hp, c, ttp, first) in bv_sched.get(g, []):
            if first and c == 2:
                emit_evac0(hp)
            emit_bv_mm(hp, c, ttp, first)
        if g <= 255:
            emit_qk(g, 2)
            emit_qk(g, 3)
            emit_exp(g, 2)
            emit_exp(g, 3)
        for gd in pv_list(g):
            emit_pv(gd)

    emit_head_boundary(15)
    for i, (c, t) in enumerate([(c, t) for t in range(16) for c in (0, 1)]):
        emit_bv_mm(15, c, t, i == 0)
    emit_evac0(15)
    for i, (c, t) in enumerate([(c, t) for t in range(16) for c in (2, 3)]):
        emit_bv_mm(15, c, t, i == 0)
    emit_evac1(15)

    attn_ve_done = cnt["VE"]
    attn_ds_done = cnt["dS"]
    attn_pe_done = cnt["PE"]
    pe_qk_last = pe_qk[(255, 3)]
    ve_mult15 = ve_mult[15]

    q_wait("gpsimd", "PE", pe_qk_last)
    for et in (3, 7):
        q("gpsimd", "dma_start", WfcS[:, 4096 * et + 2048:4096 * et + 4096],
          WfcTd[128 * et:128 * et + 128, 2048:4096], inc="dFc", incn=16)
    dw_fc = cnt["dFc"]
    q_wait("gpsimd", "PE", attn_pe_done)
    for ft in (14, 15):
        q("gpsimd", "dma_start", WprS[:, 1024 * ft:1024 * ft + 1024], WprojTd[128 * ft:128 * ft + 128, :],
          inc="dFp0", incn=16)
    dw_fp0 = cnt["dFp0"]
    dw_fp1 = None

    sc_relu = {}
    sc_rsq = {}
    ve_sq = {}
    pe_fc = {}
    pe_stats_mm = {}
    pe_bcast = {}
    ve_rows = {}
    ve_bcevac = {}
    ve_comb = {}
    ve_yevac = {}
    pe_proj = {}
    last_out = [0, 0]

    def emit_loads(k):
        bi = (k + 1) % 2
        sem = "dXa" if bi == 0 else "dXb"
        if k >= 2:
            q_wait("sync", "VE", ve_comb[k - 2])
        else:
            q_wait("sync", "VE", ve_xn)
            q_wait("sync", "dXN", dxn)
            if k == 1:
                q_wait("sync", "dS", attn_ds_done)
        for et in range(ET):
            q("sync", "dma_start", x8[bi][:, 256 * et:256 * et + 256],
              xT[128 * et:128 * et + 128, 256 * k:256 * k + 256], inc=sem, incn=16)
        for et in range(ET):
            dx_mark[k] = q("sync", "dma_start", xn8[bi][:, 256 * et:256 * et + 256],
                           xnd[0:128, 2048 * et + 256 * k:2048 * et + 256 * k + 256], inc=sem, incn=16)

    def emit_stats(k):
        for et in range(ET):
            osl = Osb[:, 2048 * et + 256 * k:2048 * et + 256 * k + 256]
            if k == 0:
                q_wait("gpsimd", "VE", ve_evac1[2 * et + 1])
                q_wait("gpsimd", "dS", ds_shift[2 * et + 1])
            if et >= 2:
                q_wait("gpsimd", "PE", pe_stats_mm[(k, et - 2)])
            gps = q("gpsimd", "tensor_tensor", sq256[et % 2][:], osl, osl,
                    op=ALU.mult, inc="GP")
            if k == 0:
                q_wait("tensor", "VE", max(ve_mult[15], ve_evac1[2 * et + 1]))
                q_wait("tensor", "dS", ds_shift[2 * et + 1])
            elif et == 0:
                q_wait("tensor", "VE", ve_rows[k - 1])
            q("tensor", "matmul", bank(6, 0, 1, 0, 256), ones[:, 0:1],
              osl,
              start=(et == 0), stop=(et == ET - 1), skip_group_check=True)
            q_wait("tensor", "GP", gps)
            q("tensor", "matmul", bank(6, 0, 1, 256, 512), ones[:, 0:1],
              sq256[et % 2][:], start=False, stop=(et == ET - 1),
              skip_group_check=True, inc="PE")
            pe_stats_mm[(k, et)] = cnt["PE"]

    def emit_rows(k):
        pi = k % 2
        ta = mu2c[pi][0:1, 0:256]
        tb = r2c[pi][0:1, 0:256]
        q_wait("vector", "PE", pe_stats_mm[(k, ET - 1)])
        if k >= 1:
            q_wait("vector", "PE", pe_bcast[k - 1])
        q("vector", "tensor_scalar", tb, bank(6, 0, 1, 0, 256),
          1.0 / E, 0.0, op0=ALU.mult, op1=ALU.add)
        q("vector", "tensor_copy", rowsb[0:1, 0:256], tb)
        q("vector", "tensor_scalar", ta, bank(6, 0, 1, 256, 512),
          1.0 / E, 0.0, op0=ALU.mult, op1=ALU.add)
        q("vector", "tensor_tensor", tb, tb, tb, op=ALU.mult)
        q("vector", "tensor_tensor", ta, ta, tb, op=ALU.subtract)
        q("vector", "tensor_single_scalar", ta, ta, LN_EPS, op=ALU.add, inc="VE")
        ve_rows[k] = cnt["VE"]
        q_wait("scalar", "VE", ve_rows[k])
        q("scalar", "activation", ta, ta, AF.Ln, inc="SC")
        sc_rsq[k] = q("scalar", "activation", rowsb[0:1, 256:512], ta, AF.Exp,
                      scale=-0.5, inc="SC")

    def emit_bcast(k):
        pi = k % 2
        q_wait("tensor", "SC", sc_rsq[k])
        if k >= 1:
            q_wait("tensor", "VE", ve_bcevac[k - 1])
        q("tensor", "matmul", bank(7, 0, 128, 0, 256), ones[0:1, 0:128],
          rowsb[0:1, 0:256], start=True, stop=True, skip_group_check=True)
        q("tensor", "matmul", bank(7, 0, 128, 256, 512), ones[0:1, 0:128],
          rowsb[0:1, 256:512], start=False, stop=True, skip_group_check=True, inc="PE")
        pe_bcast[k] = cnt["PE"]
        q_wait("vector", "PE", pe_bcast[k])
        q("vector", "tensor_copy", mu2c[pi][:, :], bank(7, 0, 128, 0, 256))
        q("vector", "tensor_copy", r2c[pi][:, :], bank(7, 0, 128, 256, 512), inc="VE")
        ve_bcevac[k] = cnt["VE"]

    def emit_fc(k, fb):
        bi = (k + 1) % 2
        if fb == 0:
            q_wait("tensor", "dFc", dw_fc)
            q_wait("tensor", "dXa" if bi == 0 else "dXb", dx_mark[k])
            if k >= 1:
                q_wait("tensor", "SC", sc_relu[(k - 1, FB - 2)])
        if fb == 1 and k >= 1:
            q_wait("tensor", "SC", sc_relu[(k - 1, FB - 1)])
        if fb >= 2:
            q_wait("tensor", "SC", sc_relu[(k, fb - 2)])
        for et in range(ET):
            q("tensor", "matmul", bank(fb % 2, 0, 128, 0, 256),
              WfcS[:, 4096 * et + 128 * fb:4096 * et + 128 * fb + 128],
              xn8[bi][:, 256 * et:256 * et + 256],
              start=(et == 0), stop=(et == ET - 1),
              inc=("PE" if et == ET - 1 else None))
        pe_fc[(k, fb)] = cnt["PE"]
        q_wait("scalar", "PE", pe_fc[(k, fb)])
        if fb >= 2:
            q_wait("scalar", "VE", ve_sq[(k, fb - 2)])
        elif k >= 1:
            q_wait("scalar", "VE", ve_sq[(k - 1, FB - 2 + fb)])
        sc_relu[(k, fb)] = q("scalar", "activation", h1p[fb % 2][:], bank(fb % 2, 0, 128, 0, 256),
                             AF.Relu, inc="SC")
        q_wait("vector", "SC", sc_relu[(k, fb)])
        ve_sq[(k, fb)] = q("vector", "tensor_tensor", h1q[fb % 4][:], h1p[fb % 2][:], h1p[fb % 2][:],
                           op=ALU.mult, inc="VE")

    def emit_proj(k, fb):
        q_wait("tensor", "VE", ve_sq[(k, fb)])
        if fb == 0:
            q_wait("tensor", "dFp0", dw_fp0)
            if k >= 1:
                q_wait("tensor", "VE", ve_yevac[k - 1])
        if fb == 16:
            q_wait("tensor", "dFp1", dw_fp1)
        for dt in range(8):
            b = 2 + dt // 2
            c0 = 256 * (dt % 2)
            q("tensor", "matmul", bank(b, 0, 128, c0, c0 + 256),
              WprS[:, 1024 * fb + 128 * dt:1024 * fb + 128 * dt + 128],
              h1q[fb % 4][:],
              start=(fb == 0 and dt % 2 == 0), stop=(fb == FB - 1),
              skip_group_check=True,
              inc=("PE" if dt == 7 else None))
        pe_proj[k] = cnt["PE"]

    def emit_combine_dt(k, dt):
        bi = (k + 1) % 2
        pi = k % 2
        if dt == 0:
            if k == 0:
                q_wait("vector", "dG", dg_all)
        q("vector", "tensor_tensor", t512[:, 0:256],
          Osb[:, 2048 * dt + 256 * k:2048 * dt + 256 * k + 256], mu2c[pi][:, :], op=ALU.subtract)
        q("vector", "tensor_tensor", t512[:, 0:256], t512[:, 0:256], r2c[pi][:, :], op=ALU.mult)
        q("vector", "tensor_scalar", s256[:], t512[:, 0:256],
          gpp[:, dt:dt + 1], bpp[:, dt:dt + 1], op0=ALU.mult, op1=ALU.add)
        ys = ybuf[:, 256 * dt:256 * dt + 256]
        q("vector", "tensor_scalar", gacc[:], ys, THETAS[0], CDELT[0], op0=ALU.is_gt, op1=ALU.mult)
        for i in range(1, 5):
            q("vector", "tensor_scalar", hbuf[:], ys, THETAS[i], CDELT[i], op0=ALU.is_gt, op1=ALU.mult)
            q("vector", "tensor_tensor", gacc[:], gacc[:], hbuf[:], op=ALU.add)
        q("vector", "tensor_tensor", hbuf[:], gacc[:], s256[:], op=ALU.mult)
        pd = "dO0" if dt % 2 == 0 else "dO1"
        if last_out[dt % 2] > 0:
            q_wait("vector", pd, last_out[dt % 2])
        ve_c = q("vector", "tensor_tensor", outt[dt % 2][:], x8[bi][:, 256 * dt:256 * dt + 256],
                 hbuf[:], op=ALU.add, inc="VE")
        q_wait("sync", "VE", ve_c)
        last_out[dt % 2] = q("sync", "dma_start",
                             outT[128 * dt:128 * dt + 128, 256 * k:256 * k + 256],
                             outt[dt % 2][:], inc=pd, incn=16)
        if dt == 7:
            ve_comb[k] = cnt["VE"]

    emit_loads(1)
    emit_stats(0)
    q_wait("gpsimd", "VE", ve_mult15)
    for ft in range(16, 32):
        q("gpsimd", "dma_start", WprS[:, 1024 * ft:1024 * ft + 1024], WprojTd[128 * ft:128 * ft + 128, :],
          inc="dFp1", incn=16)
    dw_fp1 = cnt["dFp1"]
    emit_rows(0)

    for k in range(NE):
        if k + 1 < NE:
            emit_stats(k + 1)
        for fb in range(0, 4):
            emit_fc(k, fb)
            if k == 0 and fb == 3:
                emit_bcast(0)
        for fb in range(4, FB):
            emit_fc(k, fb)
            emit_proj(k, fb - 4)
            if k >= 1 and fb >= 5 and (fb - 5) % 2 == 0 and (fb - 5) // 2 < 8:
                emit_combine_dt(k - 1, (fb - 5) // 2)
            if fb == 27 and k + 1 < NE:
                emit_rows(k + 1)
            if fb == 30:
                if k + 1 < NE:
                    emit_bcast(k + 1)
                if 1 <= k <= NE - 2:
                    emit_loads(k + 1)
        for fb in range(FB - 4, FB):
            emit_proj(k, fb)

        q_wait("vector", "PE", pe_proj[k])
        for dt in range(8):
            b = 2 + dt // 2
            c0 = 256 * (dt % 2)
            q("vector", "tensor_copy", ybuf[:, 256 * dt:256 * dt + 256], bank(b, 0, 128, c0, c0 + 256),
              inc=("VE" if dt == 7 else None))
        ve_yevac[k] = cnt["VE"]

    for dt in range(8):
        emit_combine_dt(NE - 1, dt)

    q_wait("sync", "dO0", cnt["dO0"])
    q_wait("sync", "dO1", cnt["dO1"])
    q_wait("gpsimd", "dO0", cnt["dO0"])
    q_wait("gpsimd", "dO1", cnt["dO1"])

    with es:
        with nc.allow_low_precision(reason="bf16 matmuls + fp32 accumulation sanctioned by problem tolerance"), nc.Block() as block:
            @block.tensor
            def _(e):
                for f in prog["tensor"]:
                    f(e)

            @block.vector
            def _(e):
                for f in prog["vector"]:
                    f(e)

            @block.scalar
            def _(e):
                for f in prog["scalar"]:
                    f(e)

            @block.sync
            def _(e):
                for f in prog["sync"]:
                    f(e)

            @block.gpsimd
            def _(e):
                for f in prog["gpsimd"]:
                    f(e)

    return nc


_NC_CACHE = None


def kernel(L0_attn_x, L0_attn_Wq, L0_attn_Wk, L0_attn_Wv, L0_attn_bias_table,
           L0_attn_ln_gamma, L0_attn_ln_beta, L0_ffn_W_fc, L0_ffn_W_proj):
    global _NC_CACHE
    x = np.asarray(L0_attn_x, dtype=np.float32)
    Wq = np.asarray(L0_attn_Wq, dtype=np.float32)
    Wk = np.asarray(L0_attn_Wk, dtype=np.float32)
    Wv = np.asarray(L0_attn_Wv, dtype=np.float32)
    tab = np.asarray(L0_attn_bias_table, dtype=np.float32)
    g1 = np.asarray(L0_attn_ln_gamma, dtype=np.float32)
    b1 = np.asarray(L0_attn_ln_beta, dtype=np.float32)
    Wfc = np.asarray(L0_ffn_W_fc, dtype=np.float32)
    Wp = np.asarray(L0_ffn_W_proj, dtype=np.float32)

    WqT = np.ascontiguousarray(Wq.T)
    WkT = np.ascontiguousarray(Wk.T)
    WvT = np.ascontiguousarray(Wv.T)
    WfcT = np.ascontiguousarray(Wfc.T)
    WprojT = np.ascontiguousarray(Wp.T)
    idx = np.minimum(np.arange(3968)[None, :] + 127 - np.arange(128)[:, None], 2046)
    M = np.ascontiguousarray(np.transpose(tab[idx], (2, 0, 1)))
    gcol = np.ascontiguousarray(g1.reshape(E, 1))
    bcol = np.ascontiguousarray(b1.reshape(E, 1))

    if _NC_CACHE is None:
        _NC_CACHE = build_nc()
    nc = _NC_CACHE

    in_maps = []
    for c in range(8):
        in_maps.append({
            "xT": np.ascontiguousarray(x[c].T),
            "WqT": WqT, "WkT": WkT, "WvT": WvT,
            "Mh": M, "gamma": gcol, "beta": bcol,
            "WfcT": WfcT, "WprojT": WprojT,
        })
    res = run_bass_kernel_spmd(nc, in_maps, core_ids=list(range(8)))
    out = np.stack([np.asarray(r["outT"]).T for r in res.results], axis=0)
    return np.ascontiguousarray(out.astype(np.float32))


# version 42 (final)
# speedup vs baseline: 1.2024x; 1.0034x over previous
import numpy as np
from contextlib import ExitStack
import concourse.bass as bass
import concourse.mybir as mybir
from concourse.bass_utils import run_bass_kernel_spmd

F32 = mybir.dt.float32
BF16 = mybir.dt.bfloat16
AF = mybir.ActivationFunctionType
ALU = mybir.AluOpType

S, E, H, D, FF = 2048, 1024, 16, 64, 4096
ET, ST, CH, FB, NE = 8, 16, 4, 32, 8
EXPC = 12.0
QKSCALE = float(E) ** -0.5
LN_EPS = 1e-5
NS = 3

def _snn_nspk(y):
    mem, n = 0.0, 0
    for _ in range(5):
        reset = 1.0 if (mem - 1.0 > 0) else 0.0
        mem = 0.9 * mem + y - reset
        n += 1 if (mem - 1.0 > 0) else 0
    return n

def _snn_thetas():
    brackets = [(0.24, 0.25), (0.46, 0.47), (0.68, 0.69), (0.90, 0.91), (1.08, 1.09)]
    ths = []
    for k, (lo, hi) in enumerate(brackets, start=1):
        assert _snn_nspk(lo) == k - 1 and _snn_nspk(hi) == k
        for _ in range(100):
            mid = 0.5 * (lo + hi)
            if _snn_nspk(mid) >= k:
                hi = mid
            else:
                lo = mid
        ths.append(0.5 * (lo + hi))
    return ths

THETAS = _snn_thetas()
def _silu(v):
    return v / (1.0 + np.exp(-v))
CDELT = [float(_silu(k / 5.0) - _silu((k - 1) / 5.0)) for k in range(1, 6)]

OFF_ONES = 0
OFF_CPOOL = 256
OFF_GAMMA = 288
OFF_BETA = 320
OFF_MU1 = 512
OFF_R1 = 8704
OFF_O = 16896
OFF_QT = 49664
OFF_KT = 82432
OFF_VEXT = 115200
OFF_XNT = 148480
OFF_FREE = 180864
ARENA_BYTES = 212096


def build_nc():
    nc = bass.Bass("TRN2", target_bir_lowering=False)

    xT = nc.dram_tensor("xT", [E, S], F32, kind="ExternalInput").ap()
    WqT = nc.dram_tensor("WqT", [E, E], F32, kind="ExternalInput").ap()
    WkT = nc.dram_tensor("WkT", [E, E], F32, kind="ExternalInput").ap()
    WvT = nc.dram_tensor("WvT", [E, E], F32, kind="ExternalInput").ap()
    Mh = nc.dram_tensor("Mh", [H, 128, 3968], F32, kind="ExternalInput").ap()
    gamma = nc.dram_tensor("gamma", [E, 1], F32, kind="ExternalInput").ap()
    beta = nc.dram_tensor("beta", [E, 1], F32, kind="ExternalInput").ap()
    WfcTd = nc.dram_tensor("WfcT", [E, FF], F32, kind="ExternalInput").ap()
    WprojTd = nc.dram_tensor("WprojT", [FF, E], F32, kind="ExternalInput").ap()
    outT = nc.dram_tensor("outT", [E, S], F32, kind="ExternalOutput").ap()
    xnd = nc.dram_tensor("xnd", [128, 16384], BF16, kind="Internal").ap()

    es = ExitStack()
    arena = es.enter_context(nc.sbuf_tensor("arena", [128, ARENA_BYTES], mybir.dt.uint8))
    base = nc.lookup_mloc(arena).addr

    def vt(name, shape, dtype, off):
        h = nc.virtual_sbuf_tensor(name, shape, dtype)
        nc._tensor(h.name, h.shape, h.dtype, type="SB")
        mloc = nc.lookup_mloc(h)
        mloc.allocated = True
        mloc.addr = base + off
        return h

    ones = vt("ones", [128, 128], BF16, OFF_ONES)
    cpool = vt("cpool", [128, 8], F32, OFF_CPOOL)
    gpp = vt("gpp", [128, 8], F32, OFF_GAMMA)
    bpp = vt("bpp", [128, 8], F32, OFF_BETA)
    mu1bc = vt("mu1bc", [128, 2048], BF16, OFF_MU1)
    r1bc = vt("r1bc", [128, 2048], BF16, OFF_R1)
    xpb = [vt("xpb0", [128, 2048], BF16, OFF_QT),
           vt("xpb1", [128, 2048], BF16, OFF_QT + 4096)]
    Osb = vt("Osb", [128, 16384], BF16, OFF_O)
    Wbuf = [vt("Wbuf0", [128, 8192], BF16, OFF_O),
            vt("Wbuf1", [128, 8192], BF16, OFF_O + 16384)]
    QT = vt("QT", [128, 16384], BF16, OFF_QT)
    KT = vt("KT", [128, 16384], BF16, OFF_KT)
    xping = [vt("xping0", [128, 2048], F32, OFF_QT),
             vt("xping1", [128, 2048], F32, OFF_QT + 8192),
             vt("xping2", [128, 2048], F32, OFF_QT + 16384),
             vt("xping3", [128, 2048], F32, OFF_QT + 24576)]
    sqp = [vt("sqp0", [128, 2048], BF16, OFF_KT),
           vt("sqp1", [128, 2048], BF16, OFF_KT + 4096)]
    Vext = vt("Vext", [128, 16640], BF16, OFF_VEXT)
    xnT = vt("xnT", [128, 16384], BF16, OFF_XNT)
    Mbuf = [vt("Mbuf0", [128, 3968], BF16, OFF_XNT),
            vt("Mbuf1", [128, 3968], BF16, OFF_XNT + 7936)]
    strip = [vt("strip0", [128, 2048], BF16, OFF_XNT + 15872),
             vt("strip1", [128, 2048], BF16, OFF_XNT + 19968),
             vt("strip2", [128, 2048], BF16, OFF_FREE)]
    zrowbf = vt("zrowbf", [128, 2048], BF16, OFF_XNT + 24064)
    zrowf = vt("zrowf", [128, 2048], F32, OFF_FREE + 8192)
    zbc = vt("zbc", [128, 2048], BF16, OFF_XNT + 28160)
    oddst = vt("oddst", [128, 2048], BF16, OFF_FREE + 4096)
    p0rows = vt("p0rows", [128, 4096], BF16, OFF_FREE)
    p0f32 = vt("p0f32", [128, 2048], F32, OFF_FREE + 8192)
    WfcS = vt("WfcS", [128, 32768], BF16, OFF_QT)
    WprS = vt("WprS", [128, 32768], BF16, OFF_VEXT)
    x8 = [vt("x8a", [128, 2048], F32, OFF_FREE),
          vt("x8b", [128, 2048], F32, OFF_MU1)]
    xn8 = [vt("xn8a", [128, 2048], BF16, OFF_R1),
           vt("xn8b", [128, 2048], BF16, OFF_R1 + 4096)]
    ybuf = vt("ybuf", [128, 2048], F32, OFF_FREE + 8192)
    h1p = [vt("h1p0", [128, 256], BF16, OFF_FREE + 16384),
           vt("h1p1", [128, 256], BF16, OFF_FREE + 16896)]
    h1q = [vt("h1q0", [128, 256], BF16, OFF_FREE + 17408),
           vt("h1q1", [128, 256], BF16, OFF_FREE + 17920),
           vt("h1q2", [128, 256], BF16, OFF_FREE + 30208),
           vt("h1q3", [128, 256], BF16, OFF_FREE + 30720)]
    gacc = vt("gacc", [128, 256], BF16, OFF_FREE + 18432)
    hbuf = vt("hbuf", [128, 256], BF16, OFF_FREE + 18944)
    outt = [vt("outt0", [128, 256], F32, OFF_FREE + 19456),
            vt("outt1", [128, 256], F32, OFF_FREE + 20480)]
    t512 = vt("t512", [128, 512], F32, OFF_FREE + 21504)
    s256 = vt("s256", [128, 256], BF16, OFF_FREE + 23552)
    sq256 = [vt("sq256a", [128, 256], BF16, OFF_FREE + 24064),
             vt("sq256b", [128, 256], BF16, OFF_FREE + 24576)]
    mu2c = [vt("mu2c0", [128, 256], F32, OFF_FREE + 25088),
            vt("mu2c1", [128, 256], F32, OFF_FREE + 26112)]
    r2c = [vt("r2c0", [128, 256], F32, OFF_FREE + 27136),
           vt("r2c1", [128, 256], F32, OFF_FREE + 28160)]
    rowsb = vt("rowsb", [128, 512], BF16, OFF_FREE + 29184)

    PS = es.enter_context(nc.psum_tensor("PS", [128, 4096], F32))

    sems = {}
    cnt = {}
    for name in ["dG", "dX0", "dX1", "dX2", "dX3", "dXa", "dXb", "dXN", "dWq", "dWk", "dWv",
                 "dM0", "dM1", "dFc", "dFp0", "dFp1", "dO0", "dO1", "dS",
                 "PE", "SC", "VE", "GP"]:
        sems[name] = es.enter_context(nc.semaphore(name))
        cnt[name] = 0

    ENGS = ("tensor", "vector", "scalar", "sync", "gpsimd")
    prog = {e: [] for e in ENGS}

    def q_wait(eng, sem, v):
        assert v <= cnt[sem], (eng, sem, v, cnt[sem])
        if v <= 0:
            return
        prog[eng].append(lambda e, s=sems[sem], vv=v: e.wait_ge(s, vv))

    def q(eng, meth, *args, inc=None, incn=1, **kw):
        tgt = None
        if inc is not None:
            cnt[inc] += incn
            tgt = (sems[inc], incn)
        def f(e, m=meth, a=args, k=kw, t=tgt):
            r = getattr(e, m)(*a, **k)
            if t is not None:
                r.then_inc(t[0], t[1])
        prog[eng].append(f)
        return cnt[inc] if inc else None

    def bank(b, p0=0, p1=128, c0=0, c1=512):
        return PS[p0:p1, 512 * b + c0: 512 * b + c1]

    q("vector", "memset", ones[:], 1.0)
    q("vector", "memset", cpool[:, 0:1], -EXPC, inc="VE")
    ve_consts = cnt["VE"]
    for et in range(ET):
        q("sync", "dma_start", gpp[:, et:et + 1], gamma[128 * et:128 * et + 128, 0:1], inc="dG", incn=16)
        q("sync", "dma_start", bpp[:, et:et + 1], beta[128 * et:128 * et + 128, 0:1], inc="dG", incn=16)
    dg_all = cnt["dG"]

    ve_cast = [0] * ET
    sc_sq = [0] * ET
    pe_s2 = [0] * ET
    dx_x = [0] * ET
    for et in range(ET):
        ldq = "sync" if et % 2 == 0 else "gpsimd"
        if et >= 4:
            q_wait(ldq, "VE", ve_cast[et - 4])
        dxp = "dX%d" % (et % 4)
        dx_x[et] = q(ldq, "dma_start", xping[et % 4][:], xT[128 * et:128 * et + 128, :], inc=dxp, incn=16)
        if et == 3:
            for e2 in range(ET):
                q("gpsimd", "dma_start", Wbuf[0][:, 1024 * e2:1024 * e2 + 1024],
                  WqT[128 * e2:128 * e2 + 128, :], inc="dWq", incn=16)
        q_wait("vector", dxp, dx_x[et])
        ve_cast[et] = q("vector", "tensor_copy", xnT[:, 2048 * et:2048 * et + 2048], xping[et % 4][:], inc="VE")
        q_wait("scalar", "VE", ve_cast[et])
        if et >= 2:
            q_wait("scalar", "PE", pe_s2[et - 2])
        sc_sq[et] = q("scalar", "activation", sqp[et % 2][:], xnT[:, 2048 * et:2048 * et + 2048], AF.Square, inc="SC")
        q_wait("tensor", "VE", ve_cast[et])
        for c in range(CH):
            q("tensor", "matmul", bank(c, 0, 1), ones[:, 0:1],
              xnT[:, 2048 * et + 512 * c:2048 * et + 512 * c + 512],
              start=(et == 0), stop=(et == ET - 1))
        q_wait("tensor", "SC", sc_sq[et])
        for c in range(CH):
            q("tensor", "matmul", bank(4 + c, 0, 1), ones[:, 0:1],
              sqp[et % 2][:, 512 * c:512 * c + 512],
              start=(et == 0), stop=(et == ET - 1), inc=("PE" if c == CH - 1 else None))
        pe_s2[et] = cnt["PE"]
    pe_stats = cnt["PE"]

    q_wait("vector", "PE", pe_stats)
    for c in range(CH):
        mu = p0f32[0:1, 0:512]
        m2 = p0f32[0:1, 512:1024]
        t3 = p0f32[0:1, 1024:1536]
        va = xping[2][0:1, 512 * c:512 * c + 512]
        q("vector", "tensor_scalar", mu, bank(c, 0, 1), 1.0 / E, 0.0, op0=ALU.mult, op1=ALU.add)
        q("vector", "tensor_copy", p0rows[0:1, 512 * c:512 * c + 512], mu)
        q("vector", "tensor_scalar", m2, bank(4 + c, 0, 1), 1.0 / E, 0.0, op0=ALU.mult, op1=ALU.add)
        q("vector", "tensor_tensor", t3, mu, mu, op=ALU.mult)
        q("vector", "tensor_tensor", va, m2, t3, op=ALU.subtract)
        q("vector", "tensor_single_scalar", va, va, LN_EPS, op=ALU.add,
          inc=("VE" if c == CH - 1 else None))
    ve_rows1 = cnt["VE"]
    q_wait("scalar", "VE", ve_rows1)
    q("scalar", "activation", xping[2][0:1, 0:2048], xping[2][0:1, 0:2048], AF.Ln, inc="SC")
    sc_rt1 = q("scalar", "activation", p0rows[0:1, 2048:4096], xping[2][0:1, 0:2048],
               AF.Exp, scale=-0.5, inc="SC")

    q_wait("tensor", "VE", ve_rows1)
    q_wait("tensor", "SC", sc_rt1)
    for r in range(2):
        for c in range(CH):
            q("tensor", "matmul", bank(4 * r + c), ones[0:1, 0:128],
              p0rows[0:1, 2048 * r + 512 * c:2048 * r + 512 * c + 512],
              start=True, stop=True, inc=("PE" if (r, c) == (1, CH - 1) else None))
    pe_bc1 = cnt["PE"]
    q_wait("vector", "PE", pe_bc1)
    for c in range(CH):
        q("vector", "tensor_copy", mu1bc[:, 512 * c:512 * c + 512], bank(c))
        q("vector", "tensor_copy", r1bc[:, 512 * c:512 * c + 512], bank(4 + c), inc="VE")
    ve_bc1 = cnt["VE"]

    xn_sem = [0] * ET
    xn_eng = [0] * ET
    q_wait("gpsimd", "VE", ve_bc1)
    for et in range(ET):
        eng, scr = ("vector", xpb[0]) if et < 4 else ("gpsimd", xpb[1])
        sem = "VE" if et < 4 else "GP"
        for cc in range(4):
            sl = slice(2048 * et + 512 * cc, 2048 * et + 512 * cc + 512)
            s5 = slice(512 * cc, 512 * cc + 512)
            q(eng, "tensor_tensor", scr[:, s5], xnT[:, sl], mu1bc[:, s5], op=ALU.subtract)
            xn_sem[et] = q(eng, "tensor_tensor", xnT[:, sl], scr[:, s5], r1bc[:, s5], op=ALU.mult,
                           inc=(sem if cc == 3 else None))
        xn_eng[et] = sem
    ve_xn = xn_sem[3]
    gp_xn = xn_sem[ET - 1]

    q_wait("scalar", "VE", ve_xn)
    q_wait("scalar", "GP", gp_xn)
    dxn = q("scalar", "dma_start", xnd[0:128, :], xnT[:, :], inc="dXN", incn=16)

    dx_mark = {}
    q_wait("sync", "VE", ve_xn)
    q_wait("sync", "dXN", dxn)
    for et in range(ET):
        q("sync", "dma_start", x8[1][:, 256 * et:256 * et + 256],
          xT[128 * et:128 * et + 128, 0:256], inc="dXb", incn=16)
    for et in range(ET):
        dx_mark[0] = q("sync", "dma_start", xn8[1][:, 256 * et:256 * et + 256],
                       xnd[0:128, 2048 * et:2048 * et + 256], inc="dXb", incn=16)

    dw_sem = ["dWq", "dWk", "dWv"]
    for et in range(ET):
        q("gpsimd", "dma_start", Wbuf[1][:, 1024 * et:1024 * et + 1024], WkT[128 * et:128 * et + 128, :],
          inc="dWk", incn=16)

    q("vector", "memset", Vext[:, :], 1.0)
    q("vector", "memset", oddst[:], 0.0, inc="VE")
    ve_vones = cnt["VE"]

    pe_q = 0
    ve_qevac = {}
    def qkv_mm(widx, dst_is_q, prev_evac=None):
        nonlocal pe_q
        for j in range(ET):
            for c in range(CH):
                g = j * CH + c
                b = g % 4
                q_wait("tensor", dw_sem[widx], 16 * ET)
                if g >= 4:
                    q_wait("tensor", "VE", ve_qevac[(widx, g - 4)])
                elif prev_evac is not None:
                    q_wait("tensor", "VE", prev_evac(g))
                for et in range(ET):
                    if widx == 0 and g == 0:
                        q_wait("tensor", xn_eng[et], xn_sem[et])
                    q("tensor", "matmul", bank(b),
                      Wbuf[widx % 2][:, 1024 * et + 128 * j:1024 * et + 128 * j + 128],
                      xnT[:, 2048 * et + 512 * c:2048 * et + 512 * c + 512],
                      start=(et == 0), stop=(et == ET - 1),
                      inc=("PE" if et == ET - 1 else None))
                pe_g = cnt["PE"]
                q_wait("vector", "PE", pe_g)
                dst = QT if dst_is_q else KT
                ve_qevac[(widx, g)] = q("vector", "tensor_copy",
                                        dst[:, 2048 * j + 512 * c:2048 * j + 512 * c + 512], bank(b), inc="VE")
        pe_q = cnt["PE"]

    qkv_mm(0, True)
    q_wait("gpsimd", "PE", pe_q)
    for et in range(ET):
        q("gpsimd", "dma_start", Wbuf[0][:, 1024 * et:1024 * et + 1024], WvT[128 * et:128 * et + 128, :],
          inc="dWv", incn=16)
    qkv_mm(1, False, prev_evac=lambda g: ve_qevac[(0, 28 + g)])

    ve_vevac = {}
    q_wait("tensor", "VE", ve_vones)
    for tt in range(ST):
        for c2 in range(2):
            g = tt * 2 + c2
            b = g % 4
            q_wait("tensor", "dWv", 16 * ET)
            if g >= 4:
                q_wait("tensor", "VE", ve_vevac[g - 4])
            else:
                q_wait("tensor", "VE", ve_qevac[(1, 28 + g)])
            for et in range(ET):
                q("tensor", "matmul", bank(b),
                  xnT[:, 2048 * et + 128 * tt:2048 * et + 128 * tt + 128],
                  Wbuf[0][:, 1024 * et + 512 * c2:1024 * et + 512 * c2 + 512],
                  start=(et == 0), stop=(et == ET - 1),
                  inc=("PE" if et == ET - 1 else None))
            pe_g = cnt["PE"]
            q_wait("vector", "PE", pe_g)
            for h2 in range(8):
                h = 8 * c2 + h2
                q("vector", "tensor_copy",
                  Vext[:, 1040 * h + 65 * tt:1040 * h + 65 * tt + 64],
                  bank(b, 0, 128, 64 * h2, 64 * h2 + 64),
                  inc=("VE" if h2 == 7 else None))
            ve_vevac[g] = cnt["VE"]
    pe_qkv_done = cnt["PE"]
    ve_qkv_done = cnt["VE"]

    dw_m = [0] * H
    q_wait("gpsimd", "PE", pe_qkv_done)
    q_wait("gpsimd", "dXN", dxn)
    q("gpsimd", "dma_start", Mbuf[0][:], Mh[0], inc="dM0", incn=16)
    dw_m[0] = cnt["dM0"]
    q("gpsimd", "dma_start", Mbuf[1][:], Mh[1], inc="dM1", incn=16)
    dw_m[1] = cnt["dM1"]

    sc_exp = {}
    sc_recip = {}
    pe_qk = {}
    pe_pv = {}
    pe_bv = {}
    pe_zbc = {}
    ve_mult = {}
    ve_zcopy = {}
    ve_evac0 = {}
    ve_evac1 = {}
    ds_shift = {}

    def dsthalf(hp):
        j = hp // 2
        if hp % 2 == 0:
            return Osb, 2048 * j
        return oddst, 0

    def emit_qk(g, c):
        h, tt = divmod(g, 16)
        Koff, j = 64 * (h % 2), h // 2
        if g == 0 and c == 0:
            q_wait("tensor", "VE", ve_qkv_done)
        if c < 2:
            if g >= 1:
                q_wait("tensor", "SC", sc_exp[(g - 1, c + 2)])
        else:
            q_wait("tensor", "SC", sc_exp[(g, c - 2)])
        q("tensor", "matmul", bank(c % 2),
          KT[Koff:Koff + 64, 2048 * j + 128 * tt:2048 * j + 128 * tt + 128],
          QT[Koff:Koff + 64, 2048 * j + 512 * c:2048 * j + 512 * c + 512],
          start=True, stop=True, inc="PE")
        pe_qk[(g, c)] = cnt["PE"]

    def emit_exp(g, c):
        if c == 0 and g >= NS:
            q_wait("scalar", "PE", pe_pv[g - NS])
        q_wait("scalar", "PE", pe_qk[(g, c)])
        if g == 0 and c == 0:
            q_wait("scalar", "VE", ve_consts)
        sc_exp[(g, c)] = q("scalar", "activation", strip[g % NS][:, 512 * c:512 * c + 512],
                           PS[:, 512 * (c % 2):512 * (c % 2) + 512], AF.Exp,
                           bias=cpool[:, 0:1], scale=QKSCALE, inc="SC")

    def emit_pv(gd):
        h, tt = divmod(gd, 16)
        q_wait("tensor", "SC", sc_exp[(gd, 3)])
        if tt == 0:
            q_wait("tensor", "VE", ve_mult[h - 1] if h > 0 else ve_qkv_done)
        for c in range(4):
            q("tensor", "matmul", bank(4 + c, 0, 65),
              Vext[:, 1040 * h + 65 * tt:1040 * h + 65 * tt + 65],
              strip[gd % NS][:, 512 * c:512 * c + 512],
              start=(tt == 0), stop=(tt == 15),
              inc=("PE" if c == 3 else None))
        pe_pv[gd] = cnt["PE"]

    def emit_bv_mm(hp, c, ttp, first):
        if first:
            if c < 2:
                q_wait("tensor", "VE", ve_zcopy[(hp, 1)])
                q_wait("tensor", "dM0" if hp % 2 == 0 else "dM1", dw_m[hp])
            else:
                q_wait("tensor", "VE", ve_evac0[hp])
        w0 = 512 * c - 128 * ttp + 1920
        q("tensor", "matmul", bank(2 + (c % 2), 0, 64),
          Vext[:, 1040 * hp + 65 * ttp:1040 * hp + 65 * ttp + 64],
          Mbuf[hp % 2][:, w0:w0 + 512],
          start=(ttp == 0), stop=(ttp == 15),
          inc=("PE" if (ttp == 15 and c % 2 == 1) else None))
        if ttp == 15 and c % 2 == 1:
            pe_bv[(hp, c // 2)] = cnt["PE"]

    def emit_evac0(hp):
        dst, dco = dsthalf(hp)
        q_wait("vector", "PE", pe_bv[(hp, 0)])
        q("vector", "tensor_tensor", dst[0:64, dco + 0:dco + 512],
          dst[0:64, dco + 0:dco + 512], PS[0:64, 1024:1536], op=ALU.add)
        q("vector", "tensor_tensor", dst[0:64, dco + 512:dco + 1024],
          dst[0:64, dco + 512:dco + 1024], PS[0:64, 1536:2048], op=ALU.add, inc="VE")
        ve_evac0[hp] = cnt["VE"]

    def emit_evac1(hp):
        dst, dco = dsthalf(hp)
        q_wait("vector", "PE", pe_bv[(hp, 1)])
        q("vector", "tensor_tensor", dst[0:64, dco + 1024:dco + 1536],
          dst[0:64, dco + 1024:dco + 1536], PS[0:64, 1024:1536], op=ALU.add)
        q("vector", "tensor_tensor", dst[0:64, dco + 1536:dco + 2048],
          dst[0:64, dco + 1536:dco + 2048], PS[0:64, 1536:2048], op=ALU.add, inc="VE")
        ve_evac1[hp] = cnt["VE"]
        if hp % 2 == 1:
            q_wait("sync", "VE", ve_evac1[hp])
            jj = hp // 2
            q("sync", "dma_start", Osb[64:128, 2048 * jj:2048 * jj + 2048], oddst[0:64, :],
              inc="dS", incn=16)
            ds_shift[hp] = cnt["dS"]

    def emit_boundary_scalar(hp):
        if hp >= 1:
            emit_evac1(hp - 1)
        q_wait("scalar", "PE", pe_pv[16 * hp + 15])
        for half in range(2):
            q("scalar", "activation", zrowf[64:65, 1024 * half:1024 * half + 1024],
              PS[64:65, 2048 + 1024 * half:3072 + 1024 * half], AF.Ln, inc="SC")
            sc_recip[(hp, half)] = q("scalar", "activation",
                                     zrowbf[64:65, 1024 * half:1024 * half + 1024],
                                     zrowf[64:65, 1024 * half:1024 * half + 1024],
                                     AF.Exp, scale=-1.0, inc="SC")

    def emit_boundary_tensor(hp):
        for pair in range(2):
            if pair == 0:
                q_wait("tensor", "SC", sc_recip[(hp, 0)])
                if hp >= 1:
                    q_wait("tensor", "VE", ve_evac1[hp - 1])
            else:
                q_wait("tensor", "SC", sc_recip[(hp, 1)])
                q_wait("tensor", "VE", ve_zcopy[(hp, 0)])
            for k in range(2):
                c = 2 * pair + k
                q("tensor", "matmul", bank(2 + k, 0, 64), ones[64:65, 0:64],
                  zrowbf[64:65, 512 * c:512 * c + 512], start=True, stop=True,
                  inc=("PE" if k == 1 else None))
            pe_zbc[(hp, pair)] = cnt["PE"]
            q_wait("vector", "PE", pe_zbc[(hp, pair)])
            for k in range(2):
                c = 2 * pair + k
                q("vector", "tensor_copy", zbc[0:64, 512 * c:512 * c + 512], bank(2 + k, 0, 64),
                  inc=("VE" if k == 1 else None))
            ve_zcopy[(hp, pair)] = cnt["VE"]
        dst, dco = dsthalf(hp)
        if hp % 2 == 1 and hp >= 3:
            q_wait("vector", "dS", ds_shift[hp - 2])
        for c in range(4):
            q("vector", "tensor_tensor", dst[0:64, dco + 512 * c:dco + 512 * c + 512],
              PS[0:64, 2048 + 512 * c:2048 + 512 * c + 512],
              zbc[0:64, 512 * c:512 * c + 512], op=ALU.mult,
              inc=("VE" if c == 3 else None))
        ve_mult[hp] = cnt["VE"]

    def emit_head_boundary(hp):
        emit_boundary_scalar(hp)
        emit_boundary_tensor(hp)

    bv_sched = {}
    for hp in range(H - 1):
        bb = 16 * (hp + 1)
        mms0 = [(c, t) for t in range(16) for c in (0, 1)]
        for i, (c, t) in enumerate(mms0):
            bv_sched.setdefault(bb + 2 + i // 4, []).append((hp, c, t, i == 0))
        mms1 = [(c, t) for t in range(16) for c in (2, 3)]
        for i, (c, t) in enumerate(mms1):
            gc = bb + 10 + i // 5 if i < 30 else bb + 16
            bv_sched.setdefault(gc, []).append((hp, c, t, i == 0))

    def pv_list(g):
        m = g % 16
        if g < 2 or m == 1:
            return []
        if m == 2:
            return [g - 2, g - 1]
        return [g - 1]

    for g in range(257):
        h, tt = divmod(g, 16)
        if g <= 255:
            if tt == 1 and h >= 1:
                emit_boundary_scalar(h - 1)
            emit_qk(g, 0)
            emit_qk(g, 1)
            emit_exp(g, 0)
            emit_exp(g, 1)
            if tt == 1 and h >= 1:
                emit_boundary_tensor(h - 1)
            if tt == 1 and 2 <= h <= 15:
                q_wait("gpsimd", "PE", pe_bv[(h - 2, 1)])
                mp = "dM0" if h % 2 == 0 else "dM1"
                q("gpsimd", "dma_start", Mbuf[h % 2][:], Mh[h], inc=mp, incn=16)
                dw_m[h] = cnt[mp]
                ft = h - 2
                if ft <= 13:
                    q("gpsimd", "dma_start", WprS[:, 1024 * ft:1024 * ft + 1024],
                      WprojTd[128 * ft:128 * ft + 128, :], inc="dFp0", incn=16)
                if h in (4, 8, 12):
                    bidx = h // 4 - 1
                    for et in (bidx, bidx + 4):
                        q("gpsimd", "dma_start", WfcS[:, 4096 * et:4096 * et + 4096],
                          WfcTd[128 * et:128 * et + 128, :], inc="dFc", incn=16)
                if h == 14:
                    for et in (3, 7):
                        q("gpsimd", "dma_start", WfcS[:, 4096 * et:4096 * et + 2048],
                          WfcTd[128 * et:128 * et + 128, 0:2048], inc="dFc", incn=16)
        for (hp, c, ttp, first) in bv_sched.get(g, []):
            if first and c == 2:
                emit_evac0(hp)
            emit_bv_mm(hp, c, ttp, first)
        if g <= 255:
            emit_qk(g, 2)
            emit_qk(g, 3)
            emit_exp(g, 2)
            emit_exp(g, 3)
        for gd in pv_list(g):
            emit_pv(gd)

    emit_head_boundary(15)
    for i, (c, t) in enumerate([(c, t) for t in range(16) for c in (0, 1)]):
        emit_bv_mm(15, c, t, i == 0)
    emit_evac0(15)
    for i, (c, t) in enumerate([(c, t) for t in range(16) for c in (2, 3)]):
        emit_bv_mm(15, c, t, i == 0)
    emit_evac1(15)

    attn_ve_done = cnt["VE"]
    attn_ds_done = cnt["dS"]
    attn_pe_done = cnt["PE"]
    pe_qk_last = pe_qk[(255, 3)]
    ve_mult15 = ve_mult[15]

    q_wait("gpsimd", "PE", pe_qk_last)
    for et in (3, 7):
        q("gpsimd", "dma_start", WfcS[:, 4096 * et + 2048:4096 * et + 4096],
          WfcTd[128 * et:128 * et + 128, 2048:4096], inc="dFc", incn=16)
    dw_fc = cnt["dFc"]
    q_wait("gpsimd", "PE", attn_pe_done)
    for ft in (14, 15):
        q("gpsimd", "dma_start", WprS[:, 1024 * ft:1024 * ft + 1024], WprojTd[128 * ft:128 * ft + 128, :],
          inc="dFp0", incn=16)
    dw_fp0 = cnt["dFp0"]
    dw_fp1 = None

    sc_relu = {}
    sc_rsq = {}
    ve_sq = {}
    pe_fc = {}
    pe_stats_mm = {}
    pe_bcast = {}
    ve_rows = {}
    ve_bcevac = {}
    ve_comb = {}
    ve_yevac = {}
    pe_proj = {}
    last_out = [0, 0]

    def emit_loads(k):
        bi = (k + 1) % 2
        sem = "dXa" if bi == 0 else "dXb"
        if k >= 2:
            q_wait("sync", "VE", ve_comb[k - 2])
        else:
            q_wait("sync", "VE", ve_xn)
            q_wait("sync", "dXN", dxn)
            if k == 1:
                q_wait("sync", "dS", attn_ds_done)
        for et in range(ET):
            q("sync", "dma_start", x8[bi][:, 256 * et:256 * et + 256],
              xT[128 * et:128 * et + 128, 256 * k:256 * k + 256], inc=sem, incn=16)
        for et in range(ET):
            dx_mark[k] = q("sync", "dma_start", xn8[bi][:, 256 * et:256 * et + 256],
                           xnd[0:128, 2048 * et + 256 * k:2048 * et + 256 * k + 256], inc=sem, incn=16)

    def emit_stats(k):
        for et in range(ET):
            osl = Osb[:, 2048 * et + 256 * k:2048 * et + 256 * k + 256]
            if k == 0:
                q_wait("gpsimd", "VE", ve_evac1[2 * et + 1])
                q_wait("gpsimd", "dS", ds_shift[2 * et + 1])
            if et >= 2:
                q_wait("gpsimd", "PE", pe_stats_mm[(k, et - 2)])
            gps = q("gpsimd", "tensor_tensor", sq256[et % 2][:], osl, osl,
                    op=ALU.mult, inc="GP")
            if k == 0:
                q_wait("tensor", "VE", max(ve_mult[15], ve_evac1[2 * et + 1]))
                q_wait("tensor", "dS", ds_shift[2 * et + 1])
            elif et == 0:
                q_wait("tensor", "VE", ve_rows[k - 1])
            q("tensor", "matmul", bank(6, 0, 1, 0, 256), ones[:, 0:1],
              osl,
              start=(et == 0), stop=(et == ET - 1), skip_group_check=True)
            q_wait("tensor", "GP", gps)
            q("tensor", "matmul", bank(6, 0, 1, 256, 512), ones[:, 0:1],
              sq256[et % 2][:], start=False, stop=(et == ET - 1),
              skip_group_check=True, inc="PE")
            pe_stats_mm[(k, et)] = cnt["PE"]

    def emit_rows(k):
        pi = k % 2
        ta = mu2c[pi][0:1, 0:256]
        tb = r2c[pi][0:1, 0:256]
        q_wait("vector", "PE", pe_stats_mm[(k, ET - 1)])
        if k >= 1:
            q_wait("vector", "PE", pe_bcast[k - 1])
        q("vector", "tensor_scalar", tb, bank(6, 0, 1, 0, 256),
          1.0 / E, 0.0, op0=ALU.mult, op1=ALU.add)
        q("vector", "tensor_copy", rowsb[0:1, 0:256], tb)
        q("vector", "tensor_scalar", ta, bank(6, 0, 1, 256, 512),
          1.0 / E, 0.0, op0=ALU.mult, op1=ALU.add)
        q("vector", "tensor_tensor", tb, tb, tb, op=ALU.mult)
        q("vector", "tensor_tensor", ta, ta, tb, op=ALU.subtract)
        q("vector", "tensor_single_scalar", ta, ta, LN_EPS, op=ALU.add, inc="VE")
        ve_rows[k] = cnt["VE"]
        q_wait("scalar", "VE", ve_rows[k])
        q("scalar", "activation", ta, ta, AF.Ln, inc="SC")
        sc_rsq[k] = q("scalar", "activation", rowsb[0:1, 256:512], ta, AF.Exp,
                      scale=-0.5, inc="SC")

    def emit_bcast(k):
        pi = k % 2
        q_wait("tensor", "SC", sc_rsq[k])
        if k >= 1:
            q_wait("tensor", "VE", ve_bcevac[k - 1])
        q("tensor", "matmul", bank(7, 0, 128, 0, 256), ones[0:1, 0:128],
          rowsb[0:1, 0:256], start=True, stop=True, skip_group_check=True)
        q("tensor", "matmul", bank(7, 0, 128, 256, 512), ones[0:1, 0:128],
          rowsb[0:1, 256:512], start=False, stop=True, skip_group_check=True, inc="PE")
        pe_bcast[k] = cnt["PE"]
        q_wait("vector", "PE", pe_bcast[k])
        q("vector", "tensor_copy", mu2c[pi][:, :], bank(7, 0, 128, 0, 256))
        q("vector", "tensor_copy", r2c[pi][:, :], bank(7, 0, 128, 256, 512), inc="VE")
        ve_bcevac[k] = cnt["VE"]

    def emit_fc(k, fb):
        bi = (k + 1) % 2
        if fb == 0:
            q_wait("tensor", "dFc", dw_fc)
            q_wait("tensor", "dXa" if bi == 0 else "dXb", dx_mark[k])
            if k >= 1:
                q_wait("tensor", "SC", sc_relu[(k - 1, FB - 2)])
        if fb == 1 and k >= 1:
            q_wait("tensor", "SC", sc_relu[(k - 1, FB - 1)])
        if fb >= 2:
            q_wait("tensor", "SC", sc_relu[(k, fb - 2)])
        for et in range(ET):
            q("tensor", "matmul", bank(fb % 2, 0, 128, 0, 256),
              WfcS[:, 4096 * et + 128 * fb:4096 * et + 128 * fb + 128],
              xn8[bi][:, 256 * et:256 * et + 256],
              start=(et == 0), stop=(et == ET - 1),
              inc=("PE" if et == ET - 1 else None))
        pe_fc[(k, fb)] = cnt["PE"]
        q_wait("scalar", "PE", pe_fc[(k, fb)])
        if fb >= 2:
            q_wait("scalar", "VE", ve_sq[(k, fb - 2)])
        elif k >= 1:
            q_wait("scalar", "VE", ve_sq[(k - 1, FB - 2 + fb)])
        sc_relu[(k, fb)] = q("scalar", "activation", h1p[fb % 2][:], bank(fb % 2, 0, 128, 0, 256),
                             AF.Relu, inc="SC")
        q_wait("vector", "SC", sc_relu[(k, fb)])
        ve_sq[(k, fb)] = q("vector", "tensor_tensor", h1q[fb % 4][:], h1p[fb % 2][:], h1p[fb % 2][:],
                           op=ALU.mult, inc="VE")

    def emit_proj(k, fb):
        q_wait("tensor", "VE", ve_sq[(k, fb)])
        if fb == 0:
            q_wait("tensor", "dFp0", dw_fp0)
            if k >= 1:
                q_wait("tensor", "VE", ve_yevac[k - 1])
        if fb == 16:
            q_wait("tensor", "dFp1", dw_fp1)
        for dt in range(8):
            b = 2 + dt // 2
            c0 = 256 * (dt % 2)
            q("tensor", "matmul", bank(b, 0, 128, c0, c0 + 256),
              WprS[:, 1024 * fb + 128 * dt:1024 * fb + 128 * dt + 128],
              h1q[fb % 4][:],
              start=(fb == 0 and dt % 2 == 0), stop=(fb == FB - 1),
              skip_group_check=True,
              inc=("PE" if dt == 7 else None))
        pe_proj[k] = cnt["PE"]

    def emit_combine_dt(k, dt):
        bi = (k + 1) % 2
        pi = k % 2
        if dt == 0:
            if k == 0:
                q_wait("vector", "dG", dg_all)
        q("vector", "tensor_tensor", t512[:, 0:256],
          Osb[:, 2048 * dt + 256 * k:2048 * dt + 256 * k + 256], mu2c[pi][:, :], op=ALU.subtract)
        q("vector", "tensor_tensor", t512[:, 0:256], t512[:, 0:256], r2c[pi][:, :], op=ALU.mult)
        q("vector", "tensor_scalar", s256[:], t512[:, 0:256],
          gpp[:, dt:dt + 1], bpp[:, dt:dt + 1], op0=ALU.mult, op1=ALU.add)
        ys = ybuf[:, 256 * dt:256 * dt + 256]
        q("vector", "tensor_scalar", gacc[:], ys, THETAS[0], CDELT[0], op0=ALU.is_gt, op1=ALU.mult)
        for i in range(1, 5):
            q("vector", "tensor_scalar", hbuf[:], ys, THETAS[i], CDELT[i], op0=ALU.is_gt, op1=ALU.mult)
            q("vector", "tensor_tensor", gacc[:], gacc[:], hbuf[:], op=ALU.add)
        q("vector", "tensor_tensor", hbuf[:], gacc[:], s256[:], op=ALU.mult)
        pd = "dO0" if dt % 2 == 0 else "dO1"
        if last_out[dt % 2] > 0:
            q_wait("vector", pd, last_out[dt % 2])
        ve_c = q("vector", "tensor_tensor", outt[dt % 2][:], x8[bi][:, 256 * dt:256 * dt + 256],
                 hbuf[:], op=ALU.add, inc="VE")
        q_wait("sync", "VE", ve_c)
        last_out[dt % 2] = q("sync", "dma_start",
                             outT[128 * dt:128 * dt + 128, 256 * k:256 * k + 256],
                             outt[dt % 2][:], inc=pd, incn=16)
        if dt == 7:
            ve_comb[k] = cnt["VE"]

    emit_loads(1)
    emit_stats(0)
    q_wait("gpsimd", "VE", ve_mult15)
    for ft in range(16, 32):
        q("gpsimd", "dma_start", WprS[:, 1024 * ft:1024 * ft + 1024], WprojTd[128 * ft:128 * ft + 128, :],
          inc="dFp1", incn=16)
    dw_fp1 = cnt["dFp1"]
    emit_rows(0)

    for k in range(NE):
        if k + 1 < NE:
            emit_stats(k + 1)
        for fb in range(0, 4):
            emit_fc(k, fb)
            if k == 0 and fb == 3:
                emit_bcast(0)
        for fb in range(4, FB):
            emit_fc(k, fb)
            emit_proj(k, fb - 4)
            if k >= 1 and fb >= 4 and (fb - 4) % 2 == 0 and (fb - 4) // 2 < 8:
                emit_combine_dt(k - 1, (fb - 4) // 2)
            if fb == 27 and k + 1 < NE:
                emit_rows(k + 1)
            if fb == 30:
                if k + 1 < NE:
                    emit_bcast(k + 1)
                if 1 <= k <= NE - 2:
                    emit_loads(k + 1)
        for fb in range(FB - 4, FB):
            emit_proj(k, fb)

        q_wait("vector", "PE", pe_proj[k])
        for dt in range(8):
            b = 2 + dt // 2
            c0 = 256 * (dt % 2)
            q("vector", "tensor_copy", ybuf[:, 256 * dt:256 * dt + 256], bank(b, 0, 128, c0, c0 + 256),
              inc=("VE" if dt == 7 else None))
        ve_yevac[k] = cnt["VE"]

    for dt in range(8):
        emit_combine_dt(NE - 1, dt)

    q_wait("sync", "dO0", cnt["dO0"])
    q_wait("sync", "dO1", cnt["dO1"])
    q_wait("gpsimd", "dO0", cnt["dO0"])
    q_wait("gpsimd", "dO1", cnt["dO1"])

    with es:
        with nc.allow_low_precision(reason="bf16 matmuls + fp32 accumulation sanctioned by problem tolerance"), nc.Block() as block:
            @block.tensor
            def _(e):
                for f in prog["tensor"]:
                    f(e)

            @block.vector
            def _(e):
                for f in prog["vector"]:
                    f(e)

            @block.scalar
            def _(e):
                for f in prog["scalar"]:
                    f(e)

            @block.sync
            def _(e):
                for f in prog["sync"]:
                    f(e)

            @block.gpsimd
            def _(e):
                for f in prog["gpsimd"]:
                    f(e)

    return nc


_NC_CACHE = None


def kernel(L0_attn_x, L0_attn_Wq, L0_attn_Wk, L0_attn_Wv, L0_attn_bias_table,
           L0_attn_ln_gamma, L0_attn_ln_beta, L0_ffn_W_fc, L0_ffn_W_proj):
    global _NC_CACHE
    x = np.asarray(L0_attn_x, dtype=np.float32)
    Wq = np.asarray(L0_attn_Wq, dtype=np.float32)
    Wk = np.asarray(L0_attn_Wk, dtype=np.float32)
    Wv = np.asarray(L0_attn_Wv, dtype=np.float32)
    tab = np.asarray(L0_attn_bias_table, dtype=np.float32)
    g1 = np.asarray(L0_attn_ln_gamma, dtype=np.float32)
    b1 = np.asarray(L0_attn_ln_beta, dtype=np.float32)
    Wfc = np.asarray(L0_ffn_W_fc, dtype=np.float32)
    Wp = np.asarray(L0_ffn_W_proj, dtype=np.float32)

    WqT = np.ascontiguousarray(Wq.T)
    WkT = np.ascontiguousarray(Wk.T)
    WvT = np.ascontiguousarray(Wv.T)
    WfcT = np.ascontiguousarray(Wfc.T)
    WprojT = np.ascontiguousarray(Wp.T)
    idx = np.minimum(np.arange(3968)[None, :] + 127 - np.arange(128)[:, None], 2046)
    M = np.ascontiguousarray(np.transpose(tab[idx], (2, 0, 1)))
    gcol = np.ascontiguousarray(g1.reshape(E, 1))
    bcol = np.ascontiguousarray(b1.reshape(E, 1))

    if _NC_CACHE is None:
        _NC_CACHE = build_nc()
    nc = _NC_CACHE

    in_maps = []
    for c in range(8):
        in_maps.append({
            "xT": np.ascontiguousarray(x[c].T),
            "WqT": WqT, "WkT": WkT, "WvT": WvT,
            "Mh": M, "gamma": gcol, "beta": bcol,
            "WfcT": WfcT, "WprojT": WprojT,
        })
    res = run_bass_kernel_spmd(nc, in_maps, core_ids=list(range(8)))
    out = np.stack([np.asarray(r["outT"]).T for r in res.results], axis=0)
    return np.ascontiguousarray(out.astype(np.float32))
